# version 05
# baseline (speedup 1.0000x reference)
import numpy as np
import concourse.bass as bass
import concourse.mybir as mybir
from concourse.bass_utils import run_bass_kernel_spmd

F32 = mybir.dt.float32
F32R = mybir.dt.float32r
BF16 = mybir.dt.bfloat16
AF = mybir.ActivationFunctionType
ALU = mybir.AluOpType
AX = mybir.AxisListType

NCORES = 8
NB = 2
T = 2048
D = 1024
NCH = 8
H = 16
DV = 64
NSIG = 16
NTAU = 4
E = 8
NQR = 4
DFF = 4096

BASE = 0
A_X = BASE
B_H = A_X + 65536
C0 = B_H + 32768
D0 = C0 + 65536
E0 = D0 + 16384
F0 = E0 + 25600
C_EXP = C0 + 0
C_CTX = C0 + 8192
C_Q = C0 + 24576
C_K = C0 + 28672
C_VP = C0 + 32768
C_WQ = C0 + 34944
C_VF = C0 + 36992
C_RECIP = C0 + 36992
C_BCA = C0 + 41088
D_HSC = D0
D_WO = D0
D_H2F = D0
D_XSQ = D0 + 4096
D_ROWS = D0 + 8192
E_WBC = E0
E_WFM = E0 + 8192
E_B2 = E0 + 16384
E_BC8 = E0 + 20480
E_B1 = E0 + 24576


class Sem:
    __slots__ = ("name", "count", "h")

    def __init__(self, name):
        self.name = name
        self.count = 0
        self.h = None


class Plan:
    ENGINES = ("tensor", "vector", "scalar", "sync", "gpsimd")

    def __init__(self):
        self.ops = {e: [] for e in self.ENGINES}
        self.sems = []

    def sem(self, name):
        s = Sem(name)
        self.sems.append(s)
        return s

    def op(self, eng, fn, waits=(), inc=None):
        for s, v in waits:
            assert v <= s.count, f"wait on future count: {s.name} {v}>{s.count}"
        if inc is not None:
            inc[0].count += inc[1]
        self.ops[eng].append((tuple(waits), fn, inc))

    def emit_engine(self, eng, obj):
        waited = {}
        for waits, fn, inc in self.ops[eng]:
            for s, v in waits:
                if v > 0 and waited.get(id(s), 0) < v:
                    obj.wait_ge(s.h, v)
                    waited[id(s)] = v
            inst = fn(obj)
            if inc is not None:
                inst.then_inc(inc[0].h, inc[1])


def dap(h, off, dims):
    return bass.AP(h, off, [list(d) for d in dims])


def build_program():
    nc = bass.Bass("TRN2", target_bir_lowering=False,
                   detect_race_conditions=False)

    def din(name, shape):
        return nc.dram_tensor(name, list(shape), F32, kind="ExternalInput")

    xT = din("xT", (NB, D, T))
    wqT = din("wqT", (D, D))
    wkT = din("wkT", (D, DV))
    wvT = din("wvT", (D, DV))
    woT = din("woT", (D, D))
    bqt = din("bqt", (128, 8))
    bot = din("bot", (128, 8))
    bvv = din("bvv", (DV, 1))
    wg = din("wg", (D, E))
    w1 = din("w1", (E, D, DFF))
    b1t = din("b1t", (128, 256))
    w2 = din("w2", (E, DFF, D))
    b2 = din("b2", (E, D))
    ones1 = din("ones1", (1, 128))
    ones128 = din("ones128", (128, 128))
    onecol = din("onecol", (128, 1))
    neg20 = din("neg20", (128, 1))
    epsc = din("epsc", (128, 1))
    id128 = din("id128", (128, 128))
    bc8 = din("bc8", (E, D))
    outT = nc.dram_tensor("outT", [NB, D, T], F32, kind="ExternalOutput")
    import os
    dbg_taps = bool(os.environ.get("KERNEL_DEBUG_TAPS"))
    if dbg_taps:
        dbg_xa = nc.dram_tensor("dbg_xa", [D, T], F32, kind="ExternalOutput")
        dbg_h2 = nc.dram_tensor("dbg_h2", [D, T], F32, kind="ExternalOutput")
        dbg_wfm = nc.dram_tensor("dbg_wfm", [8, T], F32, kind="ExternalOutput")
        dbg_gl = nc.dram_tensor("dbg_gl", [8, 512], F32, kind="ExternalOutput")
        dbg_gsc = nc.dram_tensor("dbg_gsc", [128, 128], F32, kind="ExternalOutput")
        dbg_h2f = nc.dram_tensor("dbg_h2f", [128, 1024], F32, kind="ExternalOutput")

    base = (nc.sbuf_base + 95) // 32 * 32
    MAP_BYTES = F0 + 4096

    def sb(name, shape, dtype, offset):
        return nc.alloc_sbuf_tensor_at(name, shape, dtype,
                                       offset=base + offset)
    x_sb = sb("x_sb", [128, 16384], F32, offset=A_X)
    h_sb = sb("h_sb", [128, 16384], BF16, offset=B_H)
    w1q = [sb("w1qA", [128, 8192], BF16, offset=C0),
           sb("w1qB", [128, 8192], BF16, offset=C0 + 16384)]
    w2q = [sb("w2qA", [128, 8192], BF16, offset=C0 + 32768),
           sb("w2qB", [128, 8192], BF16, offset=C0 + 49152)]
    exp_sb = sb("exp_sb", [128, 4096], BF16, offset=C_EXP)
    ctx_sb = sb("ctx_sb", [128, 8192], BF16, offset=C_CTX)
    q_sb = sb("q_sb", [64, 2048], BF16, offset=C_Q)
    k_sb = sb("k_sb", [64, 2048], BF16, offset=C_K)
    vplus = sb("vplus", [128, 1088], BF16, offset=C_VP)
    wq_sb = sb("wq_sb", [128, 1024], BF16, offset=C_WQ)
    v_f32 = sb("v_f32", [64, 2048], F32, offset=C_VF)
    recip = sb("recip", [1, 1024], F32, offset=C_RECIP)
    bca_sb = sb("bca_sb", [64, 1024], F32, offset=C_BCA)
    hsc = sb("hsc", [128, 8192], BF16, offset=D_HSC)
    woT_sb = sb("woT_sb", [128, 8192], BF16, offset=D_WO)
    h2f = sb("h2f", [128, 1024], F32, offset=D_H2F)
    xsq = sb("xsq", [128, 1024], F32, offset=D_XSQ)
    mrow = sb("mrow", [1, 512], F32, offset=D_ROWS)
    vrow = sb("vrow", [1, 512], F32, offset=D_ROWS + 2048)
    brow = sb("brow", [1, 512], F32, offset=D_ROWS + 4096)
    wbc = sb("wbc", [128, 2048], F32, offset=E_WBC)
    wfm = sb("wfm", [8, 2048], F32, offset=E_WFM)
    b2_sb = sb("b2_sb", [8, 1024], F32, offset=E_B2)
    bc8_sb = sb("bc8_sb", [8, 1024], F32, offset=E_BC8)
    b1_sb = sb("b1_sb", [128, 256], F32, offset=E_B1)

    fofs = [F0]

    def fal(name, shape, dtype):
        h = sb(name, list(shape), dtype, offset=fofs[0])
        per = int(np.prod(shape[1:])) * (2 if dtype == BF16 else 4)
        fofs[0] += (per + 31) // 32 * 32
        return h

    _resv_bytes = base + MAP_BYTES - nc.sbuf_base
    nc.alloc_sbuf_tensor("manual_map_reserved", [128, _resv_bytes],
                         mybir.dt.uint8)

    wg_sb = fal("wg_sb", (128, 64), F32)
    wkT_sb = fal("wkT_sb", (128, 512), BF16)
    wvT_sb = fal("wvT_sb", (128, 512), BF16)
    bq_sb = fal("bq_sb", (128, 8), F32)
    bo_sb = fal("bo_sb", (128, 8), F32)
    bv_sb = fal("bv_sb", (64, 1), F32)
    onec_sb = fal("onec_sb", (128, 1), F32)
    n20_sb = fal("n20_sb", (128, 1), F32)
    eps_sb = fal("eps_sb", (128, 1), F32)
    ones1_sb = fal("ones1_sb", (1, 128), F32)
    id_sb = fal("id_sb", (128, 128), F32)
    gsc = fal("gsc", (128, 128), F32)
    ones128_r = fal("ones128_r", (128, 128), F32R)
    x_r = sb("x_r", [128, 16384], F32R, offset=A_X)
    xsq_r = sb("xsq_r", [128, 1024], F32R, offset=D_XSQ)
    vrow_r = sb("vrow_r", [1, 512], F32R, offset=D_ROWS + 2048)
    brow_r = sb("brow_r", [1, 512], F32R, offset=D_ROWS + 4096)
    recip_r = sb("recip_r", [1, 1024], F32R, offset=C_RECIP)
    wfm_r = sb("wfm_r", [8, 2048], F32R, offset=E_WFM)
    bc8_r = sb("bc8_r", [8, 1024], F32R, offset=E_BC8)
    b2_r = sb("b2_r", [8, 1024], F32R, offset=E_B2)

    assert base + fofs[0] <= nc.sbuf_top, (
        f"SBUF overflow: {base + fofs[0]} > {nc.sbuf_top}")

    pall = nc.alloc_psum_tensor("pall", [128, 4096], F32)

    P = Plan()
    lds = P.sem("lds")
    ldw = P.sem("ldw")
    mm = P.sem("mm")
    dve = P.sem("dve")
    act = P.sem("act")
    st = P.sem("st")

    def confirm(src_ap, rows=1):
        dump = gsc[0:rows, 30:31]
        P.op("vector", lambda e, a=src_ap, o=dump: e.tensor_scalar_mul(
            o, a, 1.0), inc=(dve, 1))
        return dve.count

    S = dict(wo_done=0, wo_done_dve=0, ctx_free_dve=0, bcast_free_dve=0,
             qbank_free_mm=0, moe_all_done=0, moe_evac_done=0, h_free_mm=0,
             woT_cnt=0, attn_dve_done=0, xsq_free_mm=0)
    if dbg_taps:
        S["tap1"] = None
    x_ready = [0, 0]

    def ld(out_ap, in_ap, sem=lds, eng="sync", waits=()):
        P.op(eng, lambda e, o=out_ap, i=in_ap: e.dma_start(out=o, in_=i),
             waits=waits, inc=(sem, 16))

    ld(bq_sb[:, :], bqt.ap())
    ld(bo_sb[:, :], bot.ap())
    ld(bv_sb[:, :], bvv.ap())
    ld(onec_sb[:, :], onecol.ap())
    ld(n20_sb[:, :], neg20.ap())
    ld(eps_sb[:, :], epsc.ap())
    ld(ones1_sb[:, :], ones1.ap())
    ld(id_sb[:, :], id128.ap())
    ld(b1_sb[:, :], b1t.ap())
    ld(b2_sb[:, :], b2.ap())
    ld(bc8_sb[:, :], bc8.ap())
    ld(wg_sb[:, :], dap(wg, 0, [(8, 128), (1024, 8), (1, 8)]))
    consts_done = lds.count
    ld(ones128_r[:, :], ones128.ap(), ldw, "gpsimd")
    ld(wkT_sb[:, :], dap(wkT, 0, [(64, 128), (8192, 8), (1, 64)]), ldw, "gpsimd")
    ld(wvT_sb[:, :], dap(wvT, 0, [(64, 128), (8192, 8), (1, 64)]), ldw, "gpsimd")
    kv_w_cnt = ldw.count

    def x_load(b, waits):
        in_ap = dap(xT, b * D * T, [(2048, 128), (262144, 8), (1, 2048)])
        ld(x_sb[:, :], in_ap, lds, "sync", waits)
        return lds.count

    def wq_load(h, waits):
        in_ap = dap(wqT, h * 64, [(1024, 128), (131072, 8), (1, 64)])
        ld(wq_sb[:, (h % 2) * 512:(h % 2) * 512 + 512], in_ap, ldw, "gpsimd",
           waits)
        return ldw.count

    def emit_ln(b, gate):
        for tau in range(NTAU):
            tbase = tau * 512
            if gate:
                xw = [(dve, S["wo_done_dve"]), (mm, S["wo_done"])]
            else:
                xw = [(lds, x_ready[b]), (mm, S["moe_all_done"]),
                      (dve, S["moe_evac_done"])]
            sq_cnt = {}
            s2m_cnt = {}
            s1_done = 0
            for c in range(NCH):
                xs = x_sb[:, c * 2048 + tbase: c * 2048 + tbase + 512]
                w = list(xw)
                w.append((mm, s2m_cnt[c - 2] if c >= 2
                          else S["xsq_free_mm"]))
                dst = xsq[:, (c % 2) * 512:(c % 2) * 512 + 512]
                P.op("scalar", lambda e, o=dst, i=xs: e.activation(
                    o, i, AF.Square), waits=w, inc=(act, 1))
                sq_cnt[c] = act.count
                xs_r = x_r[:, c * 2048 + tbase: c * 2048 + tbase + 512]
                dst_r = xsq_r[:, (c % 2) * 512:(c % 2) * 512 + 512]
                P.op("tensor", lambda e, r=xs_r, fst=(c == 0), lst=(c == 7):
                     e.matmul(pall[0:128, 0:512], ones128_r[:, :], r,
                              start=fst, stop=lst),
                     waits=(list(xw) + [(lds, consts_done),
                                        (ldw, kv_w_cnt)]) if c == 0 else [],
                     inc=(mm, 1) if c == 7 else None)
                if c == 7:
                    s1_done = mm.count
                P.op("tensor", lambda e, r=dst_r, fst=(c == 0), lst=(c == 7):
                     e.matmul(pall[0:128, 512:1024], ones128_r[:, :], r,
                              start=fst, stop=lst),
                     waits=[(act, sq_cnt[c])], inc=(mm, 1))
                s2m_cnt[c] = mm.count
            s2_done = mm.count
            S["xsq_free_mm"] = s2_done
            P.op("vector", lambda e: e.tensor_scalar_mul(
                mrow[0:1, :], pall[0:1, 0:512], 1.0 / 1024.0),
                waits=[(mm, s1_done)])
            P.op("vector", lambda e: e.tensor_mul(
                brow[0:1, :], mrow[0:1, :], mrow[0:1, :]))
            P.op("vector", lambda e: e.scalar_tensor_tensor(
                vrow[0:1, :], pall[0:1, 512:1024], 1.0 / 1024.0, brow[0:1, :],
                ALU.mult, ALU.subtract), waits=[(mm, s2_done)], inc=(dve, 1))
            var_done = dve.count
            P.op("scalar", lambda e: e.activation(
                brow[0:1, :], vrow[0:1, :], AF.Ln, bias=eps_sb[0:1, 0:1]),
                waits=[(dve, var_done)], inc=(act, 1))
            ln_cnt = act.count
            P.op("vector", lambda e: e.tensor_scalar_mul(
                brow[0:1, :], brow[0:1, :], -0.5),
                waits=[(act, ln_cnt)], inc=(dve, 1))
            hl_cnt = dve.count
            P.op("scalar", lambda e: e.activation(
                vrow[0:1, :], brow[0:1, :], AF.Exp),
                waits=[(dve, hl_cnt)], inc=(act, 1))
            a_cnt = act.count
            P.op("vector", lambda e: e.scalar_tensor_tensor(
                brow[0:1, :], mrow[0:1, :], -1.0, vrow[0:1, :],
                ALU.mult, ALU.mult), waits=[(act, a_cnt)], inc=(dve, 1))
            P.op("vector", lambda e: e.scalar_tensor_tensor(
                gsc[0:1, 30:31], vrow[0:1, 511:512], 1.0,
                brow[0:1, 511:512], ALU.mult, ALU.add), inc=(dve, 1))
            b_cnt = dve.count
            P.op("tensor", lambda e: e.matmul(
                pall[0:128, 1024:1536], ones128_r[0:1, :], vrow_r[0:1, :],
                start=True, stop=True),
                waits=[(act, a_cnt), (dve, b_cnt)], inc=(mm, 1))
            P.op("tensor", lambda e: e.matmul(
                pall[0:128, 1536:2048], ones128_r[0:1, :], brow_r[0:1, :],
                start=True, stop=True), inc=(mm, 1))
            ab_done = mm.count
            h2f_free = {0: 0, 1: 0}
            gate_grp = 0
            for c in range(NCH):
                xs = x_sb[:, c * 2048 + tbase: c * 2048 + tbase + 512]
                hsl = h_sb[:, c * 2048 + tbase: c * 2048 + tbase + 512]
                slot = h2f[:, (c % 2) * 512:(c % 2) * 512 + 512]
                w = [(mm, ab_done), (mm, h2f_free[c % 2])]
                if not gate:
                    w.append((mm, S["h_free_mm"]))
                P.op("vector", lambda e, o=slot, i=xs: e.tensor_tensor(
                    o, i, pall[:, 1024:1536], ALU.mult), waits=w)
                if gate:
                    P.op("vector", lambda e, o=slot: e.tensor_tensor(
                        o, o, pall[:, 1536:2048], ALU.add))
                    P.op("vector", lambda e, o=hsl, i=slot: e.tensor_copy(
                        o, i), inc=(dve, 1))
                    hdone = dve.count
                    P.op("tensor", lambda e, r=slot,
                         lw=wg_sb[:, c * 8:(c + 1) * 8],
                         fst=(c == 0), lst=(c == 7): e.matmul(
                             pall[0:8, 2048:2560], lw, r, start=fst, stop=lst),
                         waits=[(dve, hdone)], inc=(mm, 1))
                    h2f_free[c % 2] = mm.count
                    gate_grp = mm.count
                else:
                    P.op("vector", lambda e, o=hsl, i=slot: e.tensor_tensor(
                        o, i, pall[:, 1536:2048], ALU.add), inc=(dve, 1))
            if gate:
                emit_top2(tau, gate_grp)
        return dve.count

    def emit_top2(tau, gate_grp):
        P.op("vector", lambda e: e.tensor_copy(
            xsq[0:8, 0:512], pall[0:8, 2048:2560]),
            waits=[(mm, gate_grp)], inc=(dve, 1))
        gevac = confirm(xsq[0:8, 511:512], rows=8)
        if dbg_taps and S.get("tap1", 1) is None:
            ld(dbg_gl.ap(), xsq[0:8, 0:512], st, "sync", [(dve, gevac)])
            ld(dbg_h2f.ap(), h2f[:, :], st, "sync", [])
        for j in range(4):
            P.op("tensor", lambda e, j=j: e.transpose(
                pall[0:128, 2560 + j * 8: 2568 + j * 8],
                xsq[0:8, j * 128:(j + 1) * 128], id_sb[0:8, 0:8]),
                waits=[(dve, gevac)] if j == 0 else [],
                inc=(mm, 1) if j == 3 else None)
        tp_done = mm.count
        S["xsq_free_mm"] = tp_done

        def g(j):
            return pall[0:128, 2560 + j * 8: 2568 + j * 8]

        def vop(fn, waits=()):
            P.op("vector", fn, waits=list(waits) + [(dve, dve.count)],
                 inc=(dve, 1))
            return dve.count

        for j in range(4):
            vop(lambda e, j=j: e.reduce_max(gsc[:, j:j + 1], g(j), axis=AX.X),
                [(mm, tp_done)] if j == 0 else [])
        for j in range(4):
            vop(lambda e, j=j: e.tensor_scalar(
                gsc[:, 32 + j * 8:40 + j * 8], g(j), gsc[:, j:j + 1],
                None, ALU.is_equal))
        for j in range(4):
            vop(lambda e, j=j: e.scalar_tensor_tensor(
                gsc[:, 64 + j * 8:72 + j * 8], gsc[:, 32 + j * 8:40 + j * 8],
                -1e30, g(j), ALU.mult, ALU.add))
        for j in range(4):
            vop(lambda e, j=j: e.reduce_max(
                gsc[:, 4 + j:5 + j], gsc[:, 64 + j * 8:72 + j * 8],
                axis=AX.X))
        d_done = vop(lambda e: e.tensor_sub(
            gsc[:, 8:12], gsc[:, 4:8], gsc[:, 0:4]))
        P.op("scalar", lambda e: e.activation(
            gsc[:, 12:16], gsc[:, 8:12], AF.Exp),
            waits=[(dve, d_done)], inc=(act, 1))
        e_done = act.count
        vop(lambda e: e.tensor_scalar_add(
            gsc[:, 16:20], gsc[:, 12:16], 1.0), [(act, e_done)])
        vop(lambda e: e.reciprocal(gsc[:, 20:24], gsc[:, 16:20]))
        vop(lambda e: e.tensor_mul(
            gsc[:, 24:28], gsc[:, 12:16], gsc[:, 20:24]))
        for j in range(4):
            vop(lambda e, j=j: e.tensor_scalar(
                gsc[:, 64 + j * 8:72 + j * 8], g(j), gsc[:, 4 + j:5 + j],
                None, ALU.is_equal))
        for j in range(4):
            vop(lambda e, j=j: e.tensor_scalar(
                gsc[:, 32 + j * 8:40 + j * 8], gsc[:, 32 + j * 8:40 + j * 8],
                gsc[:, 20 + j:21 + j], None, ALU.mult))
        for j in range(4):
            vop(lambda e, j=j: e.scalar_tensor_tensor(
                gsc[:, 96 + j * 8:104 + j * 8], gsc[:, 64 + j * 8:72 + j * 8],
                gsc[:, 24 + j:25 + j], gsc[:, 32 + j * 8:40 + j * 8],
                ALU.mult, ALU.add))
        w8_done = confirm(gsc[:, 127:128], rows=128)
        if dbg_taps and S.get("tap1", 1) is None:
            ld(dbg_gsc.ap(), gsc[:, :], st, "sync", [(dve, w8_done)])
            S["tap1"] = True
        for j in range(4):
            P.op("tensor", lambda e, j=j: e.transpose(
                pall[0:8, 2048 + j * 128: 2176 + j * 128],
                gsc[:, 96 + j * 8:104 + j * 8], id_sb[:, :]),
                waits=[(dve, w8_done)] if j == 0 else [],
                inc=(mm, 1) if j == 3 else None)
        w8t = mm.count
        for j in range(4):
            P.op("vector", lambda e, j=j, t=tau: e.tensor_copy(
                wfm[0:8, t * 512 + j * 128: t * 512 + (j + 1) * 128],
                pall[0:8, 2048 + j * 128: 2176 + j * 128]),
                waits=[(mm, w8t)] if j == 0 else [],
                inc=(dve, 1) if j == 3 else None)
        confirm(wfm[0:8, tau * 512 + 511: tau * 512 + 512], rows=8)

    def emit_kv(b, ln_done):
        kevac = 0
        for tau in range(NTAU):
            tbase = tau * 512
            for c in range(NCH):
                w = []
                if c == 0:
                    w = [(dve, ln_done), (ldw, kv_w_cnt), (dve, kevac)]
                P.op("tensor", lambda e, c=c, t=tbase, fst=(c == 0),
                     lst=(c == 7): e.matmul(
                         pall[0:64, 0:512], wkT_sb[:, c * 64:(c + 1) * 64],
                         h_sb[:, c * 2048 + t:c * 2048 + t + 512],
                         start=fst, stop=lst),
                     waits=w, inc=(mm, 1) if c == 7 else None)
            kg = mm.count
            for c in range(NCH):
                P.op("tensor", lambda e, c=c, t=tbase, fst=(c == 0),
                     lst=(c == 7): e.matmul(
                         pall[0:64, 512:1024], wvT_sb[:, c * 64:(c + 1) * 64],
                         h_sb[:, c * 2048 + t:c * 2048 + t + 512],
                         start=fst, stop=lst),
                     inc=(mm, 1) if c == 7 else None)
            vg = mm.count
            P.op("vector", lambda e, t=tbase: e.tensor_copy(
                k_sb[0:64, t:t + 512], pall[0:64, 0:512]),
                waits=[(mm, kg)], inc=(dve, 1))
            P.op("vector", lambda e, t=tbase: e.tensor_scalar_add(
                v_f32[0:64, t:t + 512], pall[0:64, 512:1024],
                bv_sb[0:64, 0:1]), waits=[(mm, vg)], inc=(dve, 1))
            kevac = dve.count
        v_done = dve.count
        for s in range(NSIG):
            P.op("tensor", lambda e, s=s: e.transpose(
                pall[0:128, 1024 + s * 64:1088 + s * 64],
                v_f32[0:64, s * 128:(s + 1) * 128], id_sb[0:64, 0:64]),
                waits=[(dve, v_done)] if s == 0 else [],
                inc=(mm, 1) if s == 15 else None)
        vt_done = mm.count
        for s in range(NSIG):
            P.op("vector", lambda e, s=s: e.tensor_copy(
                vplus[:, s * 68:s * 68 + 64],
                pall[0:128, 1024 + s * 64:1088 + s * 64]),
                waits=[(mm, vt_done)] if s == 0 else [])
        P.op("vector", lambda e: e.memset(
            bass.AP(vplus, 64, [[1088, 128], [68, 16], [1, 1]]), 1.0),
            inc=(dve, 1))
        return confirm(vplus[:, 1084:1085], rows=128)

    def emit_qproj(b, tp, h, wq_cnt, extra):
        for t2 in range(2):
            for kc in range(NCH):
                w = []
                if t2 == 0 and kc == 0:
                    w = [(ldw, wq_cnt)] + list(extra)
                P.op("tensor", lambda e, t2=t2, kc=kc, h=h, tp=tp,
                     fst=(kc == 0), lst=(kc == 7): e.matmul(
                         pall[0:64, 3072 + t2 * 512:3584 + t2 * 512],
                         wq_sb[:, (h % 2) * 512 + kc * 64:
                               (h % 2) * 512 + (kc + 1) * 64],
                         h_sb[:, kc * 2048 + tp * 1024 + t2 * 512:
                              kc * 2048 + tp * 1024 + (t2 + 1) * 512],
                         start=fst, stop=lst),
                     waits=w,
                     inc=(mm, 1) if (t2 == 1 and kc == 7) else None)
        qp = mm.count
        bq_ap = bq_sb[(h % 2) * 64:(h % 2) * 64 + 64, (h // 2):(h // 2) + 1]
        P.op("vector", lambda e, h=h, bq=bq_ap: e.tensor_scalar(
            q_sb[0:64, (h % 2) * 1024:(h % 2 + 1) * 1024],
            pall[0:64, 3072:4096], bq, 0.125, ALU.add, ALU.mult),
            waits=[(mm, qp)], inc=(dve, 1))
        qed = confirm(q_sb[0:64, (h % 2) * 1024 + 1023:
                           (h % 2) * 1024 + 1024], rows=64)
        return qp, qed

    def emit_head_tail(tp, h, recip_dve, qe_next):
        for t2 in range(2):
            w = [(dve, recip_dve), (dve, qe_next)] if t2 == 0 else []
            P.op("tensor", lambda e, t2=t2: e.matmul(
                pall[0:128, 3072 + t2 * 512:3584 + t2 * 512],
                ones128_r[0:1, :], recip_r[0:1, t2 * 512:(t2 + 1) * 512],
                start=True, stop=True),
                waits=w, inc=(mm, 1) if t2 == 1 else None)
        bc = mm.count
        P.op("vector", lambda e: e.tensor_copy(
            bca_sb[0:64, :], pall[0:64, 3072:4096]),
            waits=[(mm, bc)], inc=(dve, 1))
        S["bcast_free_dve"] = dve.count
        P.op("vector", lambda e, h=h: e.tensor_tensor(
            ctx_sb[(h % 2) * 64:(h % 2) * 64 + 64,
                   (h // 2) * 1024:(h // 2 + 1) * 1024],
            pall[0:64, 2048:3072], bca_sb[0:64, :], ALU.mult),
            inc=(dve, 1))
        S["ctx_free_dve"] = dve.count

    def emit_wo(b, tp):
        gi = 0
        evac = {}
        for t2 in range(2):
            for c in range(NCH):
                bank = 2048 + (gi % 2) * 512
                for kc in range(NCH):
                    w = []
                    if kc == 0:
                        w = [(dve, S["ctx_free_dve"]), (ldw, S["woT_cnt"])]
                        if gi >= 2:
                            w.append((dve, evac[gi - 2]))
                    P.op("tensor", lambda e, kc=kc, c=c, t2=t2, bank=bank,
                         fst=(kc == 0), lst=(kc == 7): e.matmul(
                             pall[0:128, bank:bank + 512],
                             woT_sb[:, kc * 1024 + c * 128:
                                    kc * 1024 + (c + 1) * 128],
                             ctx_sb[:, kc * 1024 + t2 * 512:
                                    kc * 1024 + (t2 + 1) * 512],
                             start=fst, stop=lst),
                         waits=w, inc=(mm, 1) if kc == 7 else None)
                grp = mm.count
                xsl = x_sb[:, c * 2048 + tp * 1024 + t2 * 512:
                           c * 2048 + tp * 1024 + (t2 + 1) * 512]
                P.op("vector", lambda e, bank=bank, xs=xsl, c=c:
                     e.scalar_tensor_tensor(
                         xs, pall[0:128, bank:bank + 512],
                         bo_sb[:, c:c + 1], xs, ALU.add, ALU.add),
                     waits=[(mm, grp)], inc=(dve, 1))
                evac[gi] = dve.count
                gi += 1
        S["wo_done"] = mm.count
        S["wo_done_dve"] = dve.count
        S["qbank_free_mm"] = mm.count

    def emit_attention(b, kv_done, ln_done):
        lastexp = [0, 0]
        qe = {}
        qp = {}
        wq_cnt = {0: wq_cnt_head0[b], 1: wq_cnt_head1[b]}
        pend_tail = None
        qp0, qe0 = emit_qproj(b, 0, 0, wq_cnt[0],
                              [(dve, ln_done), (dve, S["bcast_free_dve"]),
                               (mm, S["qbank_free_mm"])])
        qp[0], qe[0] = qp0, qe0
        g_idx = 0
        for tp in range(2):
            for h in range(H):
                seqi = tp * H + h
                qe_h = qe[seqi]
                if seqi + 2 < 2 * H:
                    wq_cnt[seqi + 2] = wq_load((seqi + 2) % H,
                                               [(mm, qp[seqi])])
                exp_done = {}
                for s in range(NSIG + 1):
                    if s < NSIG:
                        slot = g_idx % 2
                        base = slot * 1024
                        for t2 in range(2):
                            w = []
                            if t2 == 0:
                                w = [(act, lastexp[slot]), (dve, qe_h),
                                     (dve, kv_done)]
                            P.op("tensor", lambda e, s=s, t2=t2, h=h,
                                 base=base: e.matmul(
                                     pall[0:128, base + t2 * 512:
                                          base + (t2 + 1) * 512],
                                     k_sb[0:64, s * 128:(s + 1) * 128],
                                     q_sb[0:64, (h % 2) * 1024 + t2 * 512:
                                          (h % 2) * 1024 + (t2 + 1) * 512],
                                     start=True, stop=True),
                                 waits=w, inc=(mm, 1) if t2 == 1 else None)
                        qk = mm.count
                        P.op("scalar", lambda e, base=base, s=s: e.activation(
                            exp_sb[:, (s % 4) * 1024:(s % 4 + 1) * 1024],
                            pall[:, base:base + 1024], AF.Exp,
                            bias=n20_sb[:, 0:1]),
                            waits=[(mm, qk)], inc=(act, 1))
                        lastexp[slot] = act.count
                        exp_done[s] = act.count
                        g_idx += 1
                    if s == 1 and pend_tail is not None:
                        ptp, ph, prd, pseq = pend_tail
                        emit_head_tail(ptp, ph, prd, qe[pseq + 1])
                        pend_tail = None
                    if s >= 1:
                        sp = s - 1
                        for t2 in range(2):
                            w = []
                            if t2 == 0:
                                w = [(act, exp_done[sp])]
                                if sp == 0:
                                    w += [(dve, S["ctx_free_dve"]),
                                          (mm, S["wo_done"]),
                                          (dve, kv_done)]
                            P.op("tensor", lambda e, sp=sp, t2=t2: e.matmul(
                                pall[0:65, 2048 + t2 * 512:
                                     2048 + (t2 + 1) * 512],
                                vplus[:, sp * 68:sp * 68 + 65],
                                exp_sb[:, (sp % 4) * 1024 + t2 * 512:
                                       (sp % 4) * 1024 + (t2 + 1) * 512],
                                start=(sp == 0), stop=(sp == NSIG - 1)),
                                waits=w,
                                inc=(mm, 1) if (sp == NSIG - 1 and t2 == 1)
                                else None)
                    if s == NSIG - 1 and seqi + 1 < 2 * H:
                        tpn, hn = divmod(seqi + 1, H)
                        qpn, qen = emit_qproj(b, tpn, hn, wq_cnt[seqi + 1],
                                              [(dve, S["bcast_free_dve"])])
                        qp[seqi + 1], qe[seqi + 1] = qpn, qen
                pv_done = mm.count
                P.op("vector", lambda e: e.reciprocal(
                    recip[0:1, :], pall[64:65, 2048:3072]),
                    waits=[(mm, pv_done)], inc=(dve, 1))
                pend_tail = (tp, h, dve.count, seqi)
            ptp, ph, prd, pseq = pend_tail
            emit_head_tail(ptp, ph, prd, qe.get(pseq + 1, qe[pseq]))
            pend_tail = None
            emit_wo(b, tp)
        S["attn_dve_done"] = dve.count

    def emit_moe(b, gate_done):
        pair_loaded = {}

        def prefetch(p, waits):
            e_, qr = divmod(p, NQR)
            slot = p % 2
            ap1 = dap(w1, e_ * D * DFF + qr * 1024,
                      [(4096, 128), (524288, 8), (1, 1024)])
            P.op("gpsimd", lambda e, o=w1q[slot][:, :], i=ap1:
                 e.dma_start(out=o, in_=i), waits=waits, inc=(ldw, 16))
            ap2 = dap(w2, e_ * DFF * D + qr * 1024 * 1024,
                      [(1024, 128), (131072, 8), (1, 1024)])
            P.op("gpsimd", lambda e, o=w2q[slot][:, :], i=ap2:
                 e.dma_start(out=o, in_=i), inc=(ldw, 16))
            pair_loaded[p] = ldw.count

        prefetch(0, [(mm, S["wo_done"]), (dve, S["attn_dve_done"])])
        prefetch(1, [])

        def bcast_mm(e_, waits):
            for tau in range(NTAU):
                P.op("tensor", lambda e, e_=e_, tau=tau: e.matmul(
                    pall[0:128, 2048 + tau * 512:2048 + (tau + 1) * 512],
                    bc8_r[0:8, e_ * 128:(e_ + 1) * 128],
                    wfm_r[0:8, tau * 512:(tau + 1) * 512],
                    start=True, stop=True),
                    waits=waits if tau == 0 else [],
                    inc=(mm, 1) if tau == 3 else None)
            return mm.count

        def bcast_copy(cnt):
            for tau in range(NTAU):
                P.op("vector", lambda e, tau=tau: e.tensor_copy(
                    wbc[:, tau * 512:(tau + 1) * 512],
                    pall[0:128, 2048 + tau * 512:2048 + (tau + 1) * 512]),
                    waits=[(mm, cnt)] if tau == 0 else [],
                    inc=(dve, 1) if tau == 3 else None)

        bcast_copy(bcast_mm(0, [(dve, gate_done)]))
        pending_bcast = None
        pend_w2 = None
        h_free = {0: 0, 1: 0}
        hbank_free = {0: 0, 1: 0}
        obank_free = {0: 0, 1: 0}
        last_w1 = 0
        for u in range(E * NQR * NTAU):
            e_ = u // (NQR * NTAU)
            qr = (u // NTAU) % NQR
            tau = u % NTAU
            p = e_ * NQR + qr
            slot = p % 2
            hslot = u % 2
            stt7 = 0
            for fi in range(8):
                bank = (fi % 2) * 512
                for kc in range(NCH):
                    w = []
                    if kc == 0:
                        w = [(ldw, pair_loaded[p]),
                             (dve, hbank_free[fi % 2])]
                        if fi == 0:
                            w.append((mm, h_free[hslot]))
                            if u == 0:
                                w.append((dve, gate_done))
                    P.op("tensor", lambda e, kc=kc, fi=fi, sl=slot, tau=tau,
                         bank=bank, fst=(kc == 0), lst=(kc == 7): e.matmul(
                             pall[0:128, bank:bank + 512],
                             w1q[sl][:, kc * 1024 + fi * 128:
                                     kc * 1024 + (fi + 1) * 128],
                             h_sb[:, kc * 2048 + tau * 512:
                                  kc * 2048 + (tau + 1) * 512],
                             start=fst, stop=lst),
                         waits=w, inc=(mm, 1) if kc == 7 else None)
                grp = mm.count
                last_w1 = grp
                hso = hslot * 4096 + fi * 512
                b1ap = b1_sb[:, e_ * 32 + qr * 8 + fi:
                             e_ * 32 + qr * 8 + fi + 1]
                P.op("vector", lambda e, bank=bank, hso=hso, b1ap=b1ap,
                     tau=tau: e.scalar_tensor_tensor(
                         hsc[:, hso:hso + 512], pall[0:128, bank:bank + 512],
                         b1ap, wbc[:, tau * 512:(tau + 1) * 512],
                         ALU.add, ALU.mult), waits=[(mm, grp)])
                P.op("vector", lambda e, hso=hso: e.tensor_scalar_max(
                    hsc[:, hso:hso + 512], hsc[:, hso:hso + 512], 0.0),
                    inc=(dve, 1))
                hbank_free[fi % 2] = dve.count
                if fi == 7:
                    stt7 = confirm(hsc[:, hso + 511:hso + 512], rows=128)
            if qr == NQR - 1 and tau == NTAU - 1 and pending_bcast is not None:
                bcast_copy(pending_bcast)
                pending_bcast = None

            def emit_w2(u2, stt7_2, slot2, hslot2):
                e2 = u2 // (NQR * NTAU)
                qr2 = (u2 // NTAU) % NQR
                tau2 = u2 % NTAU
                for c in range(NCH):
                    ob = (tau2 * 8 + c) % 2
                    gbank = 1024 + ob * 512
                    seeded = (e2 == 0 and qr2 == 0)
                    if seeded:
                        P.op("tensor", lambda e, c=c, tau=tau2, gbank=gbank:
                             e.matmul(
                                 pall[0:128, gbank:gbank + 512],
                                 b2_r[0:8, c * 128:(c + 1) * 128],
                                 wfm_r[0:8, tau * 512:(tau + 1) * 512],
                                 start=True, stop=False),
                             waits=[(dve, stt7_2), (dve, obank_free[ob])])
                    for fc in range(8):
                        w = []
                        if fc == 0 and not seeded:
                            w = [(dve, stt7_2), (dve, obank_free[ob])]
                        P.op("tensor", lambda e, fc=fc, c=c, sl=slot2,
                             hs0=hslot2 * 4096, gbank=gbank,
                             fst=(fc == 0 and not seeded), lst=(fc == 7):
                             e.matmul(
                                 pall[0:128, gbank:gbank + 512],
                                 w2q[sl][:, fc * 1024 + c * 128:
                                         fc * 1024 + (c + 1) * 128],
                                 hsc[:, hs0 + fc * 512:hs0 + (fc + 1) * 512],
                                 start=fst, stop=lst),
                             waits=w, inc=(mm, 1) if fc == 7 else None)
                    grp = mm.count
                    xsl = x_sb[:, c * 2048 + tau2 * 512:
                               c * 2048 + (tau2 + 1) * 512]
                    P.op("vector", lambda e, gbank=gbank, xs=xsl:
                         e.tensor_add(
                             xs, xs, pall[0:128, gbank:gbank + 512]),
                         waits=[(mm, grp)], inc=(dve, 1))
                    obank_free[ob] = dve.count
                h_free[hslot2] = mm.count
                p2 = e2 * NQR + qr2
                if tau2 == NTAU - 1 and p2 + 2 < E * NQR:
                    prefetch(p2 + 2, [(mm, mm.count)])

            if pend_w2 is not None:
                emit_w2(*pend_w2)
            pend_w2 = (u, stt7, slot, hslot)
            if qr == NQR - 1 and tau == NTAU - 2 and e_ < E - 1:
                pending_bcast = bcast_mm(e_ + 1, [(dve, dve.count)])
        emit_w2(*pend_w2)
        S["moe_all_done"] = mm.count
        S["moe_evac_done"] = dve.count
        S["h_free_mm"] = last_w1

    wq_cnt_head0 = {}
    wq_cnt_head1 = {}
    for b in range(NB):
        if b == 0:
            x_ready[0] = x_load(0, [])
        else:
            x_ready[b] = x_load(b, [(st, st.count)])
        wq_cnt_head0[b] = wq_load(0, [(mm, S["qbank_free_mm"]),
                                      (mm, S["moe_all_done"])])
        wq_cnt_head1[b] = wq_load(1, [])
        ln1 = emit_ln(b, gate=False)
        P.op("gpsimd", lambda e: e.dma_start(
            out=woT_sb[:, :],
            in_=dap(woT, 0, [(1024, 128), (131072, 8), (1, 1024)])),
            waits=[(dve, ln1), (mm, S["moe_all_done"])], inc=(ldw, 16))
        S["woT_cnt"] = ldw.count
        kv_done = emit_kv(b, ln1)
        emit_attention(b, kv_done, ln1)
        if dbg_taps and b == 0:
            for c in range(NCH):
                ld(dap(dbg_xa, c * 128 * T, [(2048, 128), (1, 2048)]),
                   x_sb[:, c * 2048:(c + 1) * 2048], st, "sync",
                   [(dve, S["wo_done_dve"])] if c == 0 else [])
        gate_done = emit_ln(b, gate=True)
        if dbg_taps and b == 0:
            for c in range(NCH):
                ld(dap(dbg_h2, c * 128 * T, [(2048, 128), (1, 2048)]),
                   h_sb[:, c * 2048:(c + 1) * 2048], st, "gpsimd",
                   [(dve, gate_done)] if c == 0 else [])
            ld(dbg_wfm.ap(), wfm[:, :], st, "sync", [(dve, gate_done)])
        emit_moe(b, gate_done)
        for c in range(NCH):
            out_ap = dap(outT, b * D * T + c * 128 * T, [(2048, 128), (1, 2048)])
            P.op("sync", lambda e, o=out_ap, c=c: e.dma_start(
                out=o, in_=x_sb[:, c * 2048:(c + 1) * 2048]),
                waits=[(dve, S["moe_evac_done"])] if c == 0 else [],
                inc=(st, 16))
    P.op("sync", lambda e: None, waits=[(st, st.count)])

    from contextlib import ExitStack
    with ExitStack() as ctx:
        for s in P.sems:
            s.h = ctx.enter_context(nc.semaphore(s.name))
        block = ctx.enter_context(nc.Block())

        @block.tensor
        def _(t):
            P.emit_engine("tensor", t)

        @block.vector
        def _(v):
            P.emit_engine("vector", v)

        @block.scalar
        def _(sc):
            P.emit_engine("scalar", sc)

        @block.sync
        def _(s_):
            P.emit_engine("sync", s_)

        @block.gpsimd
        def _(g):
            P.emit_engine("gpsimd", g)

    return nc


_cache = {}


def kernel(L0_attn_hidden_states, L0_attn_Wq, L0_attn_bq, L0_attn_Wk,
           L0_attn_Wv, L0_attn_bv, L0_attn_Wo, L0_attn_bo, L0_ffn_Wg,
           L0_ffn_W1, L0_ffn_b1, L0_ffn_W2, L0_ffn_b2):
    x = np.asarray(L0_attn_hidden_states, dtype=np.float32)
    Wq = np.asarray(L0_attn_Wq, np.float32)
    bq = np.asarray(L0_attn_bq, np.float32)
    Wk = np.asarray(L0_attn_Wk, np.float32)
    Wv = np.asarray(L0_attn_Wv, np.float32)
    bv = np.asarray(L0_attn_bv, np.float32)
    Wo = np.asarray(L0_attn_Wo, np.float32)
    bo = np.asarray(L0_attn_bo, np.float32)
    Wg = np.asarray(L0_ffn_Wg, np.float32)
    W1 = np.asarray(L0_ffn_W1, np.float32)
    b1 = np.asarray(L0_ffn_b1, np.float32)
    W2 = np.asarray(L0_ffn_W2, np.float32)
    b2 = np.asarray(L0_ffn_b2, np.float32)

    if "nc" not in _cache:
        _cache["nc"] = build_program()
    nc = _cache["nc"]

    shared = {
        "wqT": np.ascontiguousarray(Wq.T),
        "wkT": np.ascontiguousarray(Wk.T),
        "wvT": np.ascontiguousarray(Wv.T),
        "woT": np.ascontiguousarray(Wo.T),
        "bqt": np.ascontiguousarray(bq.reshape(8, 128).T),
        "bot": np.ascontiguousarray(bo.reshape(8, 128).T),
        "bvv": np.ascontiguousarray(bv.reshape(64, 1)),
        "wg": np.ascontiguousarray(Wg),
        "w1": np.ascontiguousarray(W1),
        "b1t": np.ascontiguousarray(
            b1.reshape(8, 32, 128).transpose(2, 0, 1).reshape(128, 256)),
        "w2": np.ascontiguousarray(W2),
        "b2": np.ascontiguousarray(b2),
        "ones1": np.ones((1, 128), np.float32),
        "ones128": np.ones((128, 128), np.float32),
        "onecol": np.ones((128, 1), np.float32),
        "neg20": np.full((128, 1), -20.0, np.float32),
        "epsc": np.full((128, 1), 1e-5, np.float32),
        "id128": np.eye(128, dtype=np.float32),
        "bc8": np.repeat(np.eye(8, dtype=np.float32), 128, axis=1),
    }
    in_maps = []
    for c in range(NCORES):
        m = dict(shared)
        m["xT"] = np.ascontiguousarray(x[2 * c:2 * c + 2].transpose(0, 2, 1))
        in_maps.append(m)

    res = run_bass_kernel_spmd(nc, in_maps, core_ids=list(range(NCORES)))
    out = np.concatenate(
        [r["outT"].transpose(0, 2, 1) for r in res.results], axis=0)
    return np.ascontiguousarray(out, dtype=np.float32)


# version 06
# speedup vs baseline: 1.1596x; 1.1596x over previous
import numpy as np
import concourse.bass as bass
import concourse.mybir as mybir
from concourse.bass_utils import run_bass_kernel_spmd

F32 = mybir.dt.float32
F32R = mybir.dt.float32r
BF16 = mybir.dt.bfloat16
AF = mybir.ActivationFunctionType
ALU = mybir.AluOpType
AX = mybir.AxisListType

NCORES = 8
NB = 2
T = 2048
D = 1024
NCH = 8
H = 16
DV = 64
NSIG = 16
NTAU = 4
E = 8
NQR = 4
DFF = 4096

BASE = 0
A_X = BASE
B_H = A_X + 65536
C0 = B_H + 32768
D0 = C0 + 65536
E0 = D0 + 16384
F0 = E0 + 25600
C_EXP = C0 + 0
C_CTX = C0 + 8192
C_Q = C0 + 24576
C_K = C0 + 28672
C_VP = C0 + 32768
C_WQ = C0 + 34944
C_VF = C0 + 36992
C_RECIP = C0 + 36992
C_BCA = C0 + 41088
D_HSC = D0
D_WO = D0
D_H2F = D0
D_XSQ = D0 + 4096
D_ROWS = D0 + 8192
E_WBC = E0
E_WFM = E0 + 8192
E_B2 = E0 + 16384
E_BC8 = E0 + 20480
E_B1 = E0 + 24576


class Sem:
    __slots__ = ("name", "count", "h")

    def __init__(self, name):
        self.name = name
        self.count = 0
        self.h = None


class Plan:
    ENGINES = ("tensor", "vector", "scalar", "sync", "gpsimd")

    def __init__(self):
        self.ops = {e: [] for e in self.ENGINES}
        self.sems = []

    def sem(self, name):
        s = Sem(name)
        self.sems.append(s)
        return s

    def op(self, eng, fn, waits=(), inc=None):
        for s, v in waits:
            assert v <= s.count, f"wait on future count: {s.name} {v}>{s.count}"
        if inc is not None:
            inc[0].count += inc[1]
        self.ops[eng].append((tuple(waits), fn, inc))

    def emit_engine(self, eng, obj):
        waited = {}
        for waits, fn, inc in self.ops[eng]:
            for s, v in waits:
                if v > 0 and waited.get(id(s), 0) < v:
                    obj.wait_ge(s.h, v)
                    waited[id(s)] = v
            inst = fn(obj)
            if inc is not None:
                inst.then_inc(inc[0].h, inc[1])


def dap(h, off, dims):
    return bass.AP(h, off, [list(d) for d in dims])


def build_program():
    nc = bass.Bass("TRN2", target_bir_lowering=False,
                   detect_race_conditions=False)

    def din(name, shape):
        return nc.dram_tensor(name, list(shape), F32, kind="ExternalInput")

    xT = din("xT", (NB, D, T))
    wqT = din("wqT", (D, D))
    wkT = din("wkT", (D, DV))
    wvT = din("wvT", (D, DV))
    woT = din("woT", (D, D))
    bqt = din("bqt", (128, 8))
    bqd = din("bqd", (128, 16))
    bot = din("bot", (128, 8))
    bvv = din("bvv", (DV, 1))
    wg = din("wg", (D, E))
    w1 = din("w1", (E, D, DFF))
    b1t = din("b1t", (128, 256))
    w2 = din("w2", (E, DFF, D))
    b2 = din("b2", (E, D))
    ones1 = din("ones1", (1, 128))
    ones128 = din("ones128", (128, 128))
    onecol = din("onecol", (128, 1))
    neg20 = din("neg20", (128, 1))
    epsc = din("epsc", (128, 1))
    id128 = din("id128", (128, 128))
    bc8 = din("bc8", (E, D))
    outT = nc.dram_tensor("outT", [NB, D, T], F32, kind="ExternalOutput")
    import os
    dbg_taps = bool(os.environ.get("KERNEL_DEBUG_TAPS"))
    if dbg_taps:
        dbg_xa = nc.dram_tensor("dbg_xa", [D, T], F32, kind="ExternalOutput")
        dbg_h2 = nc.dram_tensor("dbg_h2", [D, T], F32, kind="ExternalOutput")
        dbg_wfm = nc.dram_tensor("dbg_wfm", [8, T], F32, kind="ExternalOutput")
        dbg_gl = nc.dram_tensor("dbg_gl", [8, 512], F32, kind="ExternalOutput")
        dbg_gsc = nc.dram_tensor("dbg_gsc", [128, 128], F32, kind="ExternalOutput")
        dbg_h2f = nc.dram_tensor("dbg_h2f", [128, 1024], F32, kind="ExternalOutput")

    base = (nc.sbuf_base + 95) // 32 * 32
    MAP_BYTES = F0 + 4096

    def sb(name, shape, dtype, offset):
        return nc.alloc_sbuf_tensor_at(name, shape, dtype,
                                       offset=base + offset)
    x_sb = sb("x_sb", [128, 16384], F32, offset=A_X)
    h_sb = sb("h_sb", [128, 16384], BF16, offset=B_H)
    w1q = [sb("w1qA", [128, 8192], BF16, offset=C0),
           sb("w1qB", [128, 8192], BF16, offset=C0 + 16384)]
    w2q = [sb("w2qA", [128, 8192], BF16, offset=C0 + 32768),
           sb("w2qB", [128, 8192], BF16, offset=C0 + 49152)]
    exp_sb = sb("exp_sb", [128, 4096], BF16, offset=C_EXP)
    ctx_sb = sb("ctx_sb", [128, 8192], BF16, offset=C_CTX)
    q_sb = sb("q_sb", [128, 2048], BF16, offset=C_Q)
    k_sb = sb("k_sb", [128, 2048], BF16, offset=C_K)
    vplus = sb("vplus", [128, 1088], BF16, offset=C_VP)
    wq_sb = sb("wq_sb", [128, 1024], BF16, offset=C_WQ)
    v_f32 = sb("v_f32", [64, 2048], F32, offset=C_VF)
    recip = sb("recip", [1, 1024], F32, offset=C_RECIP)
    bca_sb = sb("bca_sb", [64, 1024], F32, offset=C_BCA)
    hsc = sb("hsc", [128, 8192], BF16, offset=D_HSC)
    woT_sb = sb("woT_sb", [128, 8192], BF16, offset=D_WO)
    h2f = sb("h2f", [128, 1024], F32, offset=D_H2F)
    xsq = sb("xsq", [128, 1024], F32, offset=D_XSQ)
    mrow = sb("mrow", [1, 512], F32, offset=D_ROWS)
    vrow = sb("vrow", [1, 512], F32, offset=D_ROWS + 2048)
    brow = sb("brow", [1, 512], F32, offset=D_ROWS + 4096)
    wbc = sb("wbc", [128, 2048], F32, offset=E_WBC)
    wfm = sb("wfm", [8, 2048], F32, offset=E_WFM)
    b2_sb = sb("b2_sb", [8, 1024], F32, offset=E_B2)
    bc8_sb = sb("bc8_sb", [8, 1024], F32, offset=E_BC8)
    b1_sb = sb("b1_sb", [128, 256], F32, offset=E_B1)

    fofs = [F0]

    def fal(name, shape, dtype):
        h = sb(name, list(shape), dtype, offset=fofs[0])
        per = int(np.prod(shape[1:])) * (2 if dtype == BF16 else 4)
        fofs[0] += (per + 31) // 32 * 32
        return h

    _resv_bytes = base + MAP_BYTES - nc.sbuf_base
    nc.alloc_sbuf_tensor("manual_map_reserved", [128, _resv_bytes],
                         mybir.dt.uint8)

    wg_sb = fal("wg_sb", (128, 64), F32)
    wkT_sb = fal("wkT_sb", (128, 512), BF16)
    wvT_sb = fal("wvT_sb", (128, 512), BF16)
    bq_sb = fal("bq_sb", (128, 8), F32)
    bqd_sb = fal("bqd_sb", (128, 16), F32)
    bo_sb = fal("bo_sb", (128, 8), F32)
    bv_sb = fal("bv_sb", (64, 1), F32)
    onec_sb = fal("onec_sb", (128, 1), F32)
    n20_sb = fal("n20_sb", (128, 1), F32)
    eps_sb = fal("eps_sb", (128, 1), F32)
    ones1_sb = fal("ones1_sb", (1, 128), F32)
    id_sb = fal("id_sb", (128, 128), F32)
    gsc = fal("gsc", (128, 128), F32)
    ones128_r = fal("ones128_r", (128, 128), F32R)
    x_r = sb("x_r", [128, 16384], F32R, offset=A_X)
    xsq_r = sb("xsq_r", [128, 1024], F32R, offset=D_XSQ)
    vrow_r = sb("vrow_r", [1, 512], F32R, offset=D_ROWS + 2048)
    brow_r = sb("brow_r", [1, 512], F32R, offset=D_ROWS + 4096)
    recip_r = sb("recip_r", [1, 1024], F32R, offset=C_RECIP)
    wfm_r = sb("wfm_r", [8, 2048], F32R, offset=E_WFM)
    bc8_r = sb("bc8_r", [8, 1024], F32R, offset=E_BC8)
    b2_r = sb("b2_r", [8, 1024], F32R, offset=E_B2)

    assert base + fofs[0] <= nc.sbuf_top, (
        f"SBUF overflow: {base + fofs[0]} > {nc.sbuf_top}")

    pall = nc.alloc_psum_tensor("pall", [128, 4096], F32)

    P = Plan()
    lds = P.sem("lds")
    ldw = P.sem("ldw")
    mm = P.sem("mm")
    dve = P.sem("dve")
    act = P.sem("act")
    st = P.sem("st")

    def confirm(src_ap, rows=1):
        dump = gsc[0:rows, 30:31]
        P.op("vector", lambda e, a=src_ap, o=dump: e.tensor_scalar_mul(
            o, a, 1.0), inc=(dve, 1))
        return dve.count

    S = dict(wo_done=0, wo_done_dve=0, ctx_free_dve=0, bcast_free_dve=0,
             qbank_free_mm=0, moe_all_done=0, moe_evac_done=0, h_free_mm=0,
             woT_cnt=0, attn_dve_done=0, xsq_free_mm=0)
    if dbg_taps:
        S["tap1"] = None
    x_ready = [0, 0]

    def ld(out_ap, in_ap, sem=lds, eng="sync", waits=()):
        P.op(eng, lambda e, o=out_ap, i=in_ap: e.dma_start(out=o, in_=i),
             waits=waits, inc=(sem, 16))

    ld(bq_sb[:, :], bqt.ap())
    ld(bqd_sb[:, :], bqd.ap())
    ld(bo_sb[:, :], bot.ap())
    ld(bv_sb[:, :], bvv.ap())
    ld(onec_sb[:, :], onecol.ap())
    ld(n20_sb[:, :], neg20.ap())
    ld(eps_sb[:, :], epsc.ap())
    ld(ones1_sb[:, :], ones1.ap())
    ld(id_sb[:, :], id128.ap())
    ld(b1_sb[:, :], b1t.ap())
    ld(b2_sb[:, :], b2.ap())
    ld(bc8_sb[:, :], bc8.ap())
    ld(wg_sb[:, :], dap(wg, 0, [(8, 128), (1024, 8), (1, 8)]))
    consts_done = lds.count
    ld(ones128_r[:, :], ones128.ap(), ldw, "gpsimd")
    ld(wkT_sb[:, :], dap(wkT, 0, [(64, 128), (8192, 8), (1, 64)]), ldw, "gpsimd")
    ld(wvT_sb[:, :], dap(wvT, 0, [(64, 128), (8192, 8), (1, 64)]), ldw, "gpsimd")
    kv_w_cnt = ldw.count

    def x_load(b, waits):
        in_ap = dap(xT, b * D * T, [(2048, 128), (262144, 8), (1, 2048)])
        ld(x_sb[:, :], in_ap, lds, "sync", waits)
        return lds.count

    def wq_load(h, waits):
        in_ap = dap(wqT, h * 64, [(1024, 128), (131072, 8), (1, 64)])
        ld(wq_sb[:, (h % 2) * 512:(h % 2) * 512 + 512], in_ap, ldw, "gpsimd",
           waits)
        return ldw.count

    def emit_ln(b, gate):
        for tau in range(NTAU):
            tbase = tau * 512
            if gate:
                xw = [(dve, S["wo_done_dve"]), (mm, S["wo_done"])]
            else:
                xw = [(lds, x_ready[b]), (mm, S["moe_all_done"]),
                      (dve, S["moe_evac_done"])]
            sq_cnt = {}
            s2m_cnt = {}
            s1_done = 0
            for c in range(NCH):
                xs = x_sb[:, c * 2048 + tbase: c * 2048 + tbase + 512]
                w = list(xw)
                w.append((mm, s2m_cnt[c - 2] if c >= 2
                          else S["xsq_free_mm"]))
                dst = xsq[:, (c % 2) * 512:(c % 2) * 512 + 512]
                P.op("scalar", lambda e, o=dst, i=xs: e.activation(
                    o, i, AF.Square), waits=w, inc=(act, 1))
                sq_cnt[c] = act.count
                xs_r = x_r[:, c * 2048 + tbase: c * 2048 + tbase + 512]
                dst_r = xsq_r[:, (c % 2) * 512:(c % 2) * 512 + 512]
                P.op("tensor", lambda e, r=xs_r, fst=(c == 0), lst=(c == 7):
                     e.matmul(pall[0:128, 0:512], ones128_r[:, :], r,
                              start=fst, stop=lst),
                     waits=(list(xw) + [(lds, consts_done),
                                        (ldw, kv_w_cnt)]) if c == 0 else [],
                     inc=(mm, 1) if c == 7 else None)
                if c == 7:
                    s1_done = mm.count
                P.op("tensor", lambda e, r=dst_r, fst=(c == 0), lst=(c == 7):
                     e.matmul(pall[0:128, 512:1024], ones128_r[:, :], r,
                              start=fst, stop=lst),
                     waits=[(act, sq_cnt[c])], inc=(mm, 1))
                s2m_cnt[c] = mm.count
            s2_done = mm.count
            S["xsq_free_mm"] = s2_done
            P.op("vector", lambda e: e.tensor_scalar_mul(
                mrow[0:1, :], pall[0:1, 0:512], 1.0 / 1024.0),
                waits=[(mm, s1_done)])
            P.op("vector", lambda e: e.tensor_mul(
                brow[0:1, :], mrow[0:1, :], mrow[0:1, :]))
            P.op("vector", lambda e: e.scalar_tensor_tensor(
                vrow[0:1, :], pall[0:1, 512:1024], 1.0 / 1024.0, brow[0:1, :],
                ALU.mult, ALU.subtract), waits=[(mm, s2_done)], inc=(dve, 1))
            var_done = dve.count
            P.op("scalar", lambda e: e.activation(
                brow[0:1, :], vrow[0:1, :], AF.Ln, bias=eps_sb[0:1, 0:1]),
                waits=[(dve, var_done)], inc=(act, 1))
            ln_cnt = act.count
            P.op("vector", lambda e: e.tensor_scalar_mul(
                brow[0:1, :], brow[0:1, :], -0.5),
                waits=[(act, ln_cnt)], inc=(dve, 1))
            hl_cnt = dve.count
            P.op("scalar", lambda e: e.activation(
                vrow[0:1, :], brow[0:1, :], AF.Exp),
                waits=[(dve, hl_cnt)], inc=(act, 1))
            a_cnt = act.count
            P.op("vector", lambda e: e.scalar_tensor_tensor(
                brow[0:1, :], mrow[0:1, :], -1.0, vrow[0:1, :],
                ALU.mult, ALU.mult), waits=[(act, a_cnt)], inc=(dve, 1))
            P.op("vector", lambda e: e.scalar_tensor_tensor(
                gsc[0:1, 30:31], vrow[0:1, 511:512], 1.0,
                brow[0:1, 511:512], ALU.mult, ALU.add), inc=(dve, 1))
            b_cnt = dve.count
            P.op("tensor", lambda e: e.matmul(
                pall[0:128, 1024:1536], ones128_r[0:1, :], vrow_r[0:1, :],
                start=True, stop=True),
                waits=[(act, a_cnt), (dve, b_cnt)], inc=(mm, 1))
            P.op("tensor", lambda e: e.matmul(
                pall[0:128, 1536:2048], ones128_r[0:1, :], brow_r[0:1, :],
                start=True, stop=True), inc=(mm, 1))
            ab_done = mm.count
            h2f_free = {0: 0, 1: 0}
            gate_grp = 0
            for c in range(NCH):
                xs = x_sb[:, c * 2048 + tbase: c * 2048 + tbase + 512]
                hsl = h_sb[:, c * 2048 + tbase: c * 2048 + tbase + 512]
                slot = h2f[:, (c % 2) * 512:(c % 2) * 512 + 512]
                w = [(mm, ab_done), (mm, h2f_free[c % 2])]
                if not gate:
                    w.append((mm, S["h_free_mm"]))
                P.op("vector", lambda e, o=slot, i=xs: e.tensor_tensor(
                    o, i, pall[:, 1024:1536], ALU.mult), waits=w)
                if gate:
                    P.op("vector", lambda e, o=slot: e.tensor_tensor(
                        o, o, pall[:, 1536:2048], ALU.add))
                    P.op("vector", lambda e, o=hsl, i=slot: e.tensor_copy(
                        o, i), inc=(dve, 1))
                    hdone = dve.count
                    P.op("tensor", lambda e, r=slot,
                         lw=wg_sb[:, c * 8:(c + 1) * 8],
                         fst=(c == 0), lst=(c == 7): e.matmul(
                             pall[0:8, 2048:2560], lw, r, start=fst, stop=lst),
                         waits=[(dve, hdone)], inc=(mm, 1))
                    h2f_free[c % 2] = mm.count
                    gate_grp = mm.count
                else:
                    P.op("vector", lambda e, o=hsl, i=slot: e.tensor_tensor(
                        o, i, pall[:, 1536:2048], ALU.add), inc=(dve, 1))
            if gate:
                emit_top2(tau, gate_grp)
        return dve.count

    def emit_top2(tau, gate_grp):
        P.op("vector", lambda e: e.tensor_copy(
            xsq[0:8, 0:512], pall[0:8, 2048:2560]),
            waits=[(mm, gate_grp)], inc=(dve, 1))
        gevac = confirm(xsq[0:8, 511:512], rows=8)
        if dbg_taps and S.get("tap1", 1) is None:
            ld(dbg_gl.ap(), xsq[0:8, 0:512], st, "sync", [(dve, gevac)])
            ld(dbg_h2f.ap(), h2f[:, :], st, "sync", [])
        for j in range(4):
            P.op("tensor", lambda e, j=j: e.transpose(
                pall[0:128, 2560 + j * 8: 2568 + j * 8],
                xsq[0:8, j * 128:(j + 1) * 128], id_sb[0:8, 0:8]),
                waits=[(dve, gevac)] if j == 0 else [],
                inc=(mm, 1) if j == 3 else None)
        tp_done = mm.count
        S["xsq_free_mm"] = tp_done

        def g(j):
            return pall[0:128, 2560 + j * 8: 2568 + j * 8]

        def vop(fn, waits=()):
            P.op("vector", fn, waits=list(waits) + [(dve, dve.count)],
                 inc=(dve, 1))
            return dve.count

        for j in range(4):
            vop(lambda e, j=j: e.reduce_max(gsc[:, j:j + 1], g(j), axis=AX.X),
                [(mm, tp_done)] if j == 0 else [])
        for j in range(4):
            vop(lambda e, j=j: e.tensor_scalar(
                gsc[:, 32 + j * 8:40 + j * 8], g(j), gsc[:, j:j + 1],
                None, ALU.is_equal))
        for j in range(4):
            vop(lambda e, j=j: e.scalar_tensor_tensor(
                gsc[:, 64 + j * 8:72 + j * 8], gsc[:, 32 + j * 8:40 + j * 8],
                -1e30, g(j), ALU.mult, ALU.add))
        for j in range(4):
            vop(lambda e, j=j: e.reduce_max(
                gsc[:, 4 + j:5 + j], gsc[:, 64 + j * 8:72 + j * 8],
                axis=AX.X))
        d_done = vop(lambda e: e.tensor_sub(
            gsc[:, 8:12], gsc[:, 4:8], gsc[:, 0:4]))
        P.op("scalar", lambda e: e.activation(
            gsc[:, 12:16], gsc[:, 8:12], AF.Exp),
            waits=[(dve, d_done)], inc=(act, 1))
        e_done = act.count
        vop(lambda e: e.tensor_scalar_add(
            gsc[:, 16:20], gsc[:, 12:16], 1.0), [(act, e_done)])
        vop(lambda e: e.reciprocal(gsc[:, 20:24], gsc[:, 16:20]))
        vop(lambda e: e.tensor_mul(
            gsc[:, 24:28], gsc[:, 12:16], gsc[:, 20:24]))
        for j in range(4):
            vop(lambda e, j=j: e.tensor_scalar(
                gsc[:, 64 + j * 8:72 + j * 8], g(j), gsc[:, 4 + j:5 + j],
                None, ALU.is_equal))
        for j in range(4):
            vop(lambda e, j=j: e.tensor_scalar(
                gsc[:, 32 + j * 8:40 + j * 8], gsc[:, 32 + j * 8:40 + j * 8],
                gsc[:, 20 + j:21 + j], None, ALU.mult))
        for j in range(4):
            vop(lambda e, j=j: e.scalar_tensor_tensor(
                gsc[:, 96 + j * 8:104 + j * 8], gsc[:, 64 + j * 8:72 + j * 8],
                gsc[:, 24 + j:25 + j], gsc[:, 32 + j * 8:40 + j * 8],
                ALU.mult, ALU.add))
        w8_done = confirm(gsc[:, 127:128], rows=128)
        if dbg_taps and S.get("tap1", 1) is None:
            ld(dbg_gsc.ap(), gsc[:, :], st, "sync", [(dve, w8_done)])
            S["tap1"] = True
        for j in range(4):
            P.op("tensor", lambda e, j=j: e.transpose(
                pall[0:8, 2048 + j * 128: 2176 + j * 128],
                gsc[:, 96 + j * 8:104 + j * 8], id_sb[:, :]),
                waits=[(dve, w8_done)] if j == 0 else [],
                inc=(mm, 1) if j == 3 else None)
        w8t = mm.count
        for j in range(4):
            P.op("vector", lambda e, j=j, t=tau: e.tensor_copy(
                wfm[0:8, t * 512 + j * 128: t * 512 + (j + 1) * 128],
                pall[0:8, 2048 + j * 128: 2176 + j * 128]),
                waits=[(mm, w8t)] if j == 0 else [],
                inc=(dve, 1) if j == 3 else None)
        confirm(wfm[0:8, tau * 512 + 511: tau * 512 + 512], rows=8)

    def emit_kv(b, ln_done):
        kevac = 0
        for tau in range(NTAU):
            tbase = tau * 512
            for half in range(2):
                for c in range(NCH):
                    w = []
                    if half == 0 and c == 0:
                        w = [(dve, ln_done), (ldw, kv_w_cnt), (dve, kevac)]
                    P.op("tensor", lambda e, c=c, t=tbase, hf=half,
                         fst=(c == 0), lst=(c == 7): e.matmul(
                             pall[hf * 64:hf * 64 + 64, 0:512],
                             wkT_sb[:, c * 64:(c + 1) * 64],
                             h_sb[:, c * 2048 + t:c * 2048 + t + 512],
                             start=fst, stop=lst,
                             tile_position=(0, hf * 64)),
                         waits=w,
                         inc=(mm, 1) if (half == 1 and c == 7) else None)
            kg = mm.count
            for c in range(NCH):
                P.op("tensor", lambda e, c=c, t=tbase, fst=(c == 0),
                     lst=(c == 7): e.matmul(
                         pall[0:64, 512:1024], wvT_sb[:, c * 64:(c + 1) * 64],
                         h_sb[:, c * 2048 + t:c * 2048 + t + 512],
                         start=fst, stop=lst),
                     inc=(mm, 1) if c == 7 else None)
            vg = mm.count
            P.op("vector", lambda e, t=tbase: e.tensor_copy(
                k_sb[0:128, t:t + 512], pall[0:128, 0:512]),
                waits=[(mm, kg)], inc=(dve, 1))
            P.op("vector", lambda e, t=tbase: e.tensor_scalar_add(
                v_f32[0:64, t:t + 512], pall[0:64, 512:1024],
                bv_sb[0:64, 0:1]), waits=[(mm, vg)], inc=(dve, 1))
            kevac = dve.count
        v_done = dve.count
        for s in range(NSIG):
            P.op("tensor", lambda e, s=s: e.transpose(
                pall[0:128, 1024 + s * 64:1088 + s * 64],
                v_f32[0:64, s * 128:(s + 1) * 128], id_sb[0:64, 0:64]),
                waits=[(dve, v_done)] if s == 0 else [],
                inc=(mm, 1) if s == 15 else None)
        vt_done = mm.count
        for s in range(NSIG):
            P.op("vector", lambda e, s=s: e.tensor_copy(
                vplus[:, s * 68:s * 68 + 64],
                pall[0:128, 1024 + s * 64:1088 + s * 64]),
                waits=[(mm, vt_done)] if s == 0 else [])
        P.op("vector", lambda e: e.memset(
            bass.AP(vplus, 64, [[1088, 128], [68, 16], [1, 1]]), 1.0),
            inc=(dve, 1))
        return confirm(vplus[:, 1084:1085], rows=128)

    def emit_qproj(b, tp, h, wq_cnt, extra):
        for t2 in range(2):
            for half in range(2):
                for kc in range(NCH):
                    w = []
                    if t2 == 0 and half == 0 and kc == 0:
                        w = [(ldw, wq_cnt)] + list(extra)
                    P.op("tensor", lambda e, t2=t2, kc=kc, h=h, tp=tp,
                         hf=half, fst=(kc == 0), lst=(kc == 7): e.matmul(
                             pall[hf * 64:hf * 64 + 64,
                                  3072 + t2 * 512:3584 + t2 * 512],
                             wq_sb[:, (h % 2) * 512 + kc * 64:
                                   (h % 2) * 512 + (kc + 1) * 64],
                             h_sb[:, kc * 2048 + tp * 1024 + t2 * 512:
                                  kc * 2048 + tp * 1024 + (t2 + 1) * 512],
                             start=fst, stop=lst,
                             tile_position=(0, hf * 64)),
                         waits=w,
                         inc=(mm, 1) if (t2 == 1 and half == 1 and kc == 7)
                         else None)
        qp = mm.count
        bq_ap = bqd_sb[:, h:h + 1]
        P.op("vector", lambda e, h=h, bq=bq_ap: e.tensor_scalar(
            q_sb[0:128, (h % 2) * 1024:(h % 2 + 1) * 1024],
            pall[0:128, 3072:4096], bq, 0.125, ALU.add, ALU.mult),
            waits=[(mm, qp)], inc=(dve, 1))
        qed = confirm(q_sb[0:128, (h % 2) * 1024 + 1023:
                           (h % 2) * 1024 + 1024], rows=128)
        return qp, qed

    def emit_head_tail(tp, h, recip_dve, qe_next):
        for t2 in range(2):
            w = [(dve, recip_dve), (dve, qe_next)] if t2 == 0 else []
            P.op("tensor", lambda e, t2=t2: e.matmul(
                pall[0:128, 3072 + t2 * 512:3584 + t2 * 512],
                ones128_r[0:1, :], recip_r[0:1, t2 * 512:(t2 + 1) * 512],
                start=True, stop=True),
                waits=w, inc=(mm, 1) if t2 == 1 else None)
        bc = mm.count
        P.op("vector", lambda e: e.tensor_copy(
            bca_sb[0:64, :], pall[0:64, 3072:4096]),
            waits=[(mm, bc)], inc=(dve, 1))
        S["bcast_free_dve"] = dve.count
        P.op("vector", lambda e, h=h: e.tensor_tensor(
            ctx_sb[(h % 2) * 64:(h % 2) * 64 + 64,
                   (h // 2) * 1024:(h // 2 + 1) * 1024],
            pall[0:64, 2048:3072], bca_sb[0:64, :], ALU.mult),
            inc=(dve, 1))
        S["ctx_free_dve"] = dve.count

    def emit_wo(b, tp):
        gi = 0
        evac = {}
        for t2 in range(2):
            for c in range(NCH):
                bank = 2048 + (gi % 2) * 512
                for kc in range(NCH):
                    w = []
                    if kc == 0:
                        w = [(dve, S["ctx_free_dve"]), (ldw, S["woT_cnt"])]
                        if gi >= 2:
                            w.append((dve, evac[gi - 2]))
                    P.op("tensor", lambda e, kc=kc, c=c, t2=t2, bank=bank,
                         fst=(kc == 0), lst=(kc == 7): e.matmul(
                             pall[0:128, bank:bank + 512],
                             woT_sb[:, kc * 1024 + c * 128:
                                    kc * 1024 + (c + 1) * 128],
                             ctx_sb[:, kc * 1024 + t2 * 512:
                                    kc * 1024 + (t2 + 1) * 512],
                             start=fst, stop=lst),
                         waits=w, inc=(mm, 1) if kc == 7 else None)
                grp = mm.count
                xsl = x_sb[:, c * 2048 + tp * 1024 + t2 * 512:
                           c * 2048 + tp * 1024 + (t2 + 1) * 512]
                P.op("vector", lambda e, bank=bank, xs=xsl, c=c:
                     e.scalar_tensor_tensor(
                         xs, pall[0:128, bank:bank + 512],
                         bo_sb[:, c:c + 1], xs, ALU.add, ALU.add),
                     waits=[(mm, grp)], inc=(dve, 1))
                evac[gi] = dve.count
                gi += 1
        S["wo_done"] = mm.count
        S["wo_done_dve"] = dve.count
        S["qbank_free_mm"] = mm.count

    def emit_attention(b, kv_done, ln_done):
        lastexp = [0, 0]
        qe = {}
        qp = {}
        wq_cnt = {0: wq_cnt_head0[b], 1: wq_cnt_head1[b]}
        pend_tail = None
        qp0, qe0 = emit_qproj(b, 0, 0, wq_cnt[0],
                              [(dve, ln_done), (dve, S["bcast_free_dve"]),
                               (mm, S["qbank_free_mm"])])
        qp[0], qe[0] = qp0, qe0
        g_idx = 0
        for tp in range(2):
            for h in range(H):
                seqi = tp * H + h
                qe_h = qe[seqi]
                if seqi + 2 < 2 * H:
                    wq_cnt[seqi + 2] = wq_load((seqi + 2) % H,
                                               [(mm, qp[seqi])])
                exp_done = {}
                for s in range(NSIG + 1):
                    if s < NSIG:
                        slot = g_idx % 2
                        base = slot * 1024
                        for t2 in range(2):
                            w = []
                            if t2 == 0:
                                w = [(act, lastexp[slot]), (dve, qe_h),
                                     (dve, kv_done)]
                            rg = (s % 2) * 64
                            P.op("tensor", lambda e, s=s, t2=t2, h=h,
                                 base=base, rg=rg: e.matmul(
                                     pall[0:128, base + t2 * 512:
                                          base + (t2 + 1) * 512],
                                     k_sb[rg:rg + 64, s * 128:(s + 1) * 128],
                                     q_sb[rg:rg + 64,
                                          (h % 2) * 1024 + t2 * 512:
                                          (h % 2) * 1024 + (t2 + 1) * 512],
                                     start=True, stop=True,
                                     tile_position=(rg, 0)),
                                 waits=w, inc=(mm, 1) if t2 == 1 else None)
                        qk = mm.count
                        P.op("scalar", lambda e, base=base, s=s: e.activation(
                            exp_sb[:, (s % 4) * 1024:(s % 4 + 1) * 1024],
                            pall[:, base:base + 1024], AF.Exp,
                            bias=n20_sb[:, 0:1]),
                            waits=[(mm, qk)], inc=(act, 1))
                        lastexp[slot] = act.count
                        exp_done[s] = act.count
                        g_idx += 1
                    if s == 1 and pend_tail is not None:
                        ptp, ph, prd, pseq = pend_tail
                        emit_head_tail(ptp, ph, prd, qe[pseq + 1])
                        pend_tail = None
                    if s >= 1:
                        sp = s - 1
                        for t2 in range(2):
                            w = []
                            if t2 == 0:
                                w = [(act, exp_done[sp])]
                                if sp == 0:
                                    w += [(dve, S["ctx_free_dve"]),
                                          (mm, S["wo_done"]),
                                          (dve, kv_done)]
                            P.op("tensor", lambda e, sp=sp, t2=t2: e.matmul(
                                pall[0:65, 2048 + t2 * 512:
                                     2048 + (t2 + 1) * 512],
                                vplus[:, sp * 68:sp * 68 + 65],
                                exp_sb[:, (sp % 4) * 1024 + t2 * 512:
                                       (sp % 4) * 1024 + (t2 + 1) * 512],
                                start=(sp == 0), stop=(sp == NSIG - 1)),
                                waits=w,
                                inc=(mm, 1) if (sp == NSIG - 1 and t2 == 1)
                                else None)
                    if s == NSIG - 1 and seqi + 1 < 2 * H:
                        tpn, hn = divmod(seqi + 1, H)
                        qpn, qen = emit_qproj(b, tpn, hn, wq_cnt[seqi + 1],
                                              [(dve, S["bcast_free_dve"])])
                        qp[seqi + 1], qe[seqi + 1] = qpn, qen
                pv_done = mm.count
                P.op("vector", lambda e: e.reciprocal(
                    recip[0:1, :], pall[64:65, 2048:3072]),
                    waits=[(mm, pv_done)], inc=(dve, 1))
                pend_tail = (tp, h, dve.count, seqi)
            ptp, ph, prd, pseq = pend_tail
            emit_head_tail(ptp, ph, prd, qe.get(pseq + 1, qe[pseq]))
            pend_tail = None
            emit_wo(b, tp)
        S["attn_dve_done"] = dve.count

    def emit_moe(b, gate_done):
        pair_loaded = {}

        def prefetch(p, waits):
            e_, qr = divmod(p, NQR)
            slot = p % 2
            ap1 = dap(w1, e_ * D * DFF + qr * 1024,
                      [(4096, 128), (524288, 8), (1, 1024)])
            P.op("gpsimd", lambda e, o=w1q[slot][:, :], i=ap1:
                 e.dma_start(out=o, in_=i), waits=waits, inc=(ldw, 16))
            ap2 = dap(w2, e_ * DFF * D + qr * 1024 * 1024,
                      [(1024, 128), (131072, 8), (1, 1024)])
            P.op("gpsimd", lambda e, o=w2q[slot][:, :], i=ap2:
                 e.dma_start(out=o, in_=i), inc=(ldw, 16))
            pair_loaded[p] = ldw.count

        prefetch(0, [(mm, S["wo_done"]), (dve, S["attn_dve_done"])])
        prefetch(1, [])

        def bcast_mm(e_, waits):
            for tau in range(NTAU):
                P.op("tensor", lambda e, e_=e_, tau=tau: e.matmul(
                    pall[0:128, 2048 + tau * 512:2048 + (tau + 1) * 512],
                    bc8_r[0:8, e_ * 128:(e_ + 1) * 128],
                    wfm_r[0:8, tau * 512:(tau + 1) * 512],
                    start=True, stop=True),
                    waits=waits if tau == 0 else [],
                    inc=(mm, 1) if tau == 3 else None)
            return mm.count

        def bcast_copy(cnt):
            for tau in range(NTAU):
                P.op("vector", lambda e, tau=tau: e.tensor_copy(
                    wbc[:, tau * 512:(tau + 1) * 512],
                    pall[0:128, 2048 + tau * 512:2048 + (tau + 1) * 512]),
                    waits=[(mm, cnt)] if tau == 0 else [],
                    inc=(dve, 1) if tau == 3 else None)

        bcast_copy(bcast_mm(0, [(dve, gate_done)]))
        pending_bcast = None
        pend_w2 = None
        h_free = {0: 0, 1: 0}
        hbank_free = {0: 0, 1: 0}
        obank_free = {0: 0, 1: 0}
        last_w1 = 0
        for u in range(E * NQR * NTAU):
            e_ = u // (NQR * NTAU)
            qr = (u // NTAU) % NQR
            tau = u % NTAU
            p = e_ * NQR + qr
            slot = p % 2
            hslot = u % 2
            stt7 = 0
            for fi in range(8):
                bank = (fi % 2) * 512
                for kc in range(NCH):
                    w = []
                    if kc == 0:
                        w = [(ldw, pair_loaded[p]),
                             (dve, hbank_free[fi % 2])]
                        if fi == 0:
                            w.append((mm, h_free[hslot]))
                            if u == 0:
                                w.append((dve, gate_done))
                    P.op("tensor", lambda e, kc=kc, fi=fi, sl=slot, tau=tau,
                         bank=bank, fst=(kc == 0), lst=(kc == 7): e.matmul(
                             pall[0:128, bank:bank + 512],
                             w1q[sl][:, kc * 1024 + fi * 128:
                                     kc * 1024 + (fi + 1) * 128],
                             h_sb[:, kc * 2048 + tau * 512:
                                  kc * 2048 + (tau + 1) * 512],
                             start=fst, stop=lst),
                         waits=w, inc=(mm, 1) if kc == 7 else None)
                grp = mm.count
                last_w1 = grp
                hso = hslot * 4096 + fi * 512
                b1ap = b1_sb[:, e_ * 32 + qr * 8 + fi:
                             e_ * 32 + qr * 8 + fi + 1]
                P.op("vector", lambda e, bank=bank, hso=hso, b1ap=b1ap,
                     tau=tau: e.scalar_tensor_tensor(
                         hsc[:, hso:hso + 512], pall[0:128, bank:bank + 512],
                         b1ap, wbc[:, tau * 512:(tau + 1) * 512],
                         ALU.add, ALU.mult), waits=[(mm, grp)])
                P.op("vector", lambda e, hso=hso: e.tensor_scalar_max(
                    hsc[:, hso:hso + 512], hsc[:, hso:hso + 512], 0.0),
                    inc=(dve, 1))
                hbank_free[fi % 2] = dve.count
                if fi == 7:
                    stt7 = confirm(hsc[:, hso + 511:hso + 512], rows=128)
            if qr == NQR - 1 and tau == NTAU - 1 and pending_bcast is not None:
                bcast_copy(pending_bcast)
                pending_bcast = None

            def emit_w2(u2, stt7_2, slot2, hslot2):
                e2 = u2 // (NQR * NTAU)
                qr2 = (u2 // NTAU) % NQR
                tau2 = u2 % NTAU
                for c in range(NCH):
                    ob = (tau2 * 8 + c) % 2
                    gbank = 1024 + ob * 512
                    seeded = (e2 == 0 and qr2 == 0)
                    if seeded:
                        P.op("tensor", lambda e, c=c, tau=tau2, gbank=gbank:
                             e.matmul(
                                 pall[0:128, gbank:gbank + 512],
                                 b2_r[0:8, c * 128:(c + 1) * 128],
                                 wfm_r[0:8, tau * 512:(tau + 1) * 512],
                                 start=True, stop=False),
                             waits=[(dve, stt7_2), (dve, obank_free[ob])])
                    for fc in range(8):
                        w = []
                        if fc == 0 and not seeded:
                            w = [(dve, stt7_2), (dve, obank_free[ob])]
                        P.op("tensor", lambda e, fc=fc, c=c, sl=slot2,
                             hs0=hslot2 * 4096, gbank=gbank,
                             fst=(fc == 0 and not seeded), lst=(fc == 7):
                             e.matmul(
                                 pall[0:128, gbank:gbank + 512],
                                 w2q[sl][:, fc * 1024 + c * 128:
                                         fc * 1024 + (c + 1) * 128],
                                 hsc[:, hs0 + fc * 512:hs0 + (fc + 1) * 512],
                                 start=fst, stop=lst),
                             waits=w, inc=(mm, 1) if fc == 7 else None)
                    grp = mm.count
                    xsl = x_sb[:, c * 2048 + tau2 * 512:
                               c * 2048 + (tau2 + 1) * 512]
                    P.op("vector", lambda e, gbank=gbank, xs=xsl:
                         e.tensor_add(
                             xs, xs, pall[0:128, gbank:gbank + 512]),
                         waits=[(mm, grp)], inc=(dve, 1))
                    obank_free[ob] = dve.count
                h_free[hslot2] = mm.count
                p2 = e2 * NQR + qr2
                if tau2 == NTAU - 1 and p2 + 2 < E * NQR:
                    prefetch(p2 + 2, [(mm, mm.count)])

            if pend_w2 is not None:
                emit_w2(*pend_w2)
            pend_w2 = (u, stt7, slot, hslot)
            if qr == NQR - 1 and tau == NTAU - 2 and e_ < E - 1:
                pending_bcast = bcast_mm(e_ + 1, [(dve, dve.count)])
        emit_w2(*pend_w2)
        S["moe_all_done"] = mm.count
        S["moe_evac_done"] = dve.count
        S["h_free_mm"] = last_w1

    wq_cnt_head0 = {}
    wq_cnt_head1 = {}
    for b in range(NB):
        if b == 0:
            x_ready[0] = x_load(0, [])
        else:
            x_ready[b] = x_load(b, [(st, st.count)])
        wq_cnt_head0[b] = wq_load(0, [(mm, S["qbank_free_mm"]),
                                      (mm, S["moe_all_done"])])
        wq_cnt_head1[b] = wq_load(1, [])
        ln1 = emit_ln(b, gate=False)
        P.op("gpsimd", lambda e: e.dma_start(
            out=woT_sb[:, :],
            in_=dap(woT, 0, [(1024, 128), (131072, 8), (1, 1024)])),
            waits=[(dve, ln1), (mm, S["moe_all_done"])], inc=(ldw, 16))
        S["woT_cnt"] = ldw.count
        kv_done = emit_kv(b, ln1)
        emit_attention(b, kv_done, ln1)
        if dbg_taps and b == 0:
            for c in range(NCH):
                ld(dap(dbg_xa, c * 128 * T, [(2048, 128), (1, 2048)]),
                   x_sb[:, c * 2048:(c + 1) * 2048], st, "sync",
                   [(dve, S["wo_done_dve"])] if c == 0 else [])
        gate_done = emit_ln(b, gate=True)
        if dbg_taps and b == 0:
            for c in range(NCH):
                ld(dap(dbg_h2, c * 128 * T, [(2048, 128), (1, 2048)]),
                   h_sb[:, c * 2048:(c + 1) * 2048], st, "gpsimd",
                   [(dve, gate_done)] if c == 0 else [])
            ld(dbg_wfm.ap(), wfm[:, :], st, "sync", [(dve, gate_done)])
        emit_moe(b, gate_done)
        for c in range(NCH):
            out_ap = dap(outT, b * D * T + c * 128 * T, [(2048, 128), (1, 2048)])
            P.op("sync", lambda e, o=out_ap, c=c: e.dma_start(
                out=o, in_=x_sb[:, c * 2048:(c + 1) * 2048]),
                waits=[(dve, S["moe_evac_done"])] if c == 0 else [],
                inc=(st, 16))
    P.op("sync", lambda e: None, waits=[(st, st.count)])

    from contextlib import ExitStack
    with ExitStack() as ctx:
        for s in P.sems:
            s.h = ctx.enter_context(nc.semaphore(s.name))
        block = ctx.enter_context(nc.Block())

        @block.tensor
        def _(t):
            P.emit_engine("tensor", t)

        @block.vector
        def _(v):
            P.emit_engine("vector", v)

        @block.scalar
        def _(sc):
            P.emit_engine("scalar", sc)

        @block.sync
        def _(s_):
            P.emit_engine("sync", s_)

        @block.gpsimd
        def _(g):
            P.emit_engine("gpsimd", g)

    return nc


_cache = {}


def kernel(L0_attn_hidden_states, L0_attn_Wq, L0_attn_bq, L0_attn_Wk,
           L0_attn_Wv, L0_attn_bv, L0_attn_Wo, L0_attn_bo, L0_ffn_Wg,
           L0_ffn_W1, L0_ffn_b1, L0_ffn_W2, L0_ffn_b2):
    x = np.asarray(L0_attn_hidden_states, dtype=np.float32)
    Wq = np.asarray(L0_attn_Wq, np.float32)
    bq = np.asarray(L0_attn_bq, np.float32)
    Wk = np.asarray(L0_attn_Wk, np.float32)
    Wv = np.asarray(L0_attn_Wv, np.float32)
    bv = np.asarray(L0_attn_bv, np.float32)
    Wo = np.asarray(L0_attn_Wo, np.float32)
    bo = np.asarray(L0_attn_bo, np.float32)
    Wg = np.asarray(L0_ffn_Wg, np.float32)
    W1 = np.asarray(L0_ffn_W1, np.float32)
    b1 = np.asarray(L0_ffn_b1, np.float32)
    W2 = np.asarray(L0_ffn_W2, np.float32)
    b2 = np.asarray(L0_ffn_b2, np.float32)

    if "nc" not in _cache:
        _cache["nc"] = build_program()
    nc = _cache["nc"]

    shared = {
        "wqT": np.ascontiguousarray(Wq.T),
        "wkT": np.ascontiguousarray(Wk.T),
        "wvT": np.ascontiguousarray(Wv.T),
        "woT": np.ascontiguousarray(Wo.T),
        "bqt": np.ascontiguousarray(bq.reshape(8, 128).T),
        "bqd": np.ascontiguousarray(
            np.tile(bq.reshape(16, 64).T, (2, 1))),
        "bot": np.ascontiguousarray(bo.reshape(8, 128).T),
        "bvv": np.ascontiguousarray(bv.reshape(64, 1)),
        "wg": np.ascontiguousarray(Wg),
        "w1": np.ascontiguousarray(W1),
        "b1t": np.ascontiguousarray(
            b1.reshape(8, 32, 128).transpose(2, 0, 1).reshape(128, 256)),
        "w2": np.ascontiguousarray(W2),
        "b2": np.ascontiguousarray(b2),
        "ones1": np.ones((1, 128), np.float32),
        "ones128": np.ones((128, 128), np.float32),
        "onecol": np.ones((128, 1), np.float32),
        "neg20": np.full((128, 1), -20.0, np.float32),
        "epsc": np.full((128, 1), 1e-5, np.float32),
        "id128": np.eye(128, dtype=np.float32),
        "bc8": np.repeat(np.eye(8, dtype=np.float32), 128, axis=1),
    }
    in_maps = []
    for c in range(NCORES):
        m = dict(shared)
        m["xT"] = np.ascontiguousarray(x[2 * c:2 * c + 2].transpose(0, 2, 1))
        in_maps.append(m)

    res = run_bass_kernel_spmd(nc, in_maps, core_ids=list(range(NCORES)))
    out = np.concatenate(
        [r["outT"].transpose(0, 2, 1) for r in res.results], axis=0)
    return np.ascontiguousarray(out, dtype=np.float32)


# version 07
# speedup vs baseline: 1.2206x; 1.0527x over previous
import numpy as np
import concourse.bass as bass
import concourse.mybir as mybir
from concourse.bass_utils import run_bass_kernel_spmd

F32 = mybir.dt.float32
F32R = mybir.dt.float32r
BF16 = mybir.dt.bfloat16
AF = mybir.ActivationFunctionType
ALU = mybir.AluOpType
AX = mybir.AxisListType

NCORES = 8
NB = 2
T = 2048
D = 1024
NCH = 8
H = 16
DV = 64
NSIG = 16
NTAU = 4
E = 8
NQR = 4
DFF = 4096

BASE = 0
A_X = BASE
B_H = A_X + 65536
C0 = B_H + 32768
D0 = C0 + 65536
E0 = D0 + 16384
F0 = E0 + 25600
C_EXP = C0 + 0
C_CTX = C0 + 8192
C_Q = C0 + 24576
C_K = C0 + 28672
C_VP = C0 + 32768
C_WQ = C0 + 34944
C_VF = C0 + 36992
C_RECIP = C0 + 36992
C_BCA = C0 + 41088
D_HSC = D0
D_WO = D0
D_H2F = D0
D_XSQ = D0 + 4096
D_ROWS = D0 + 8192
E_WBC = E0
E_WFM = E0 + 8192
E_B2 = E0 + 16384
E_BC8 = E0 + 20480
E_B1 = E0 + 24576


class Sem:
    __slots__ = ("name", "count", "h")

    def __init__(self, name):
        self.name = name
        self.count = 0
        self.h = None


class Plan:
    ENGINES = ("tensor", "vector", "scalar", "sync", "gpsimd")

    def __init__(self):
        self.ops = {e: [] for e in self.ENGINES}
        self.sems = []

    def sem(self, name):
        s = Sem(name)
        self.sems.append(s)
        return s

    def op(self, eng, fn, waits=(), inc=None):
        for s, v in waits:
            assert v <= s.count, f"wait on future count: {s.name} {v}>{s.count}"
        if inc is not None:
            inc[0].count += inc[1]
        self.ops[eng].append((tuple(waits), fn, inc))

    def emit_engine(self, eng, obj):
        waited = {}
        for waits, fn, inc in self.ops[eng]:
            for s, v in waits:
                if v > 0 and waited.get(id(s), 0) < v:
                    obj.wait_ge(s.h, v)
                    waited[id(s)] = v
            inst = fn(obj)
            if inc is not None:
                inst.then_inc(inc[0].h, inc[1])


def dap(h, off, dims):
    return bass.AP(h, off, [list(d) for d in dims])


def build_program():
    nc = bass.Bass("TRN2", target_bir_lowering=False,
                   detect_race_conditions=False)

    def din(name, shape):
        return nc.dram_tensor(name, list(shape), F32, kind="ExternalInput")

    xT = din("xT", (NB, D, T))
    wqT = din("wqT", (D, D))
    wkT = din("wkT", (D, DV))
    wvT = din("wvT", (D, DV))
    woT = din("woT", (D, D))
    bqt = din("bqt", (128, 8))
    bqd = din("bqd", (128, 16))
    bot = din("bot", (128, 8))
    bvv = din("bvv", (DV, 1))
    wg = din("wg", (D, E))
    w1 = din("w1", (E, D, DFF))
    b1t = din("b1t", (128, 256))
    w2 = din("w2", (E, DFF, D))
    b2 = din("b2", (E, D))
    ones1 = din("ones1", (1, 128))
    ones128 = din("ones128", (128, 128))
    onecol = din("onecol", (128, 1))
    neg20 = din("neg20", (128, 1))
    epsc = din("epsc", (128, 1))
    id128 = din("id128", (128, 128))
    bc8 = din("bc8", (E, D))
    outT = nc.dram_tensor("outT", [NB, D, T], F32, kind="ExternalOutput")
    import os
    dbg_taps = bool(os.environ.get("KERNEL_DEBUG_TAPS"))
    if dbg_taps:
        dbg_xa = nc.dram_tensor("dbg_xa", [D, T], F32, kind="ExternalOutput")
        dbg_h2 = nc.dram_tensor("dbg_h2", [D, T], F32, kind="ExternalOutput")
        dbg_wfm = nc.dram_tensor("dbg_wfm", [8, T], F32, kind="ExternalOutput")
        dbg_gl = nc.dram_tensor("dbg_gl", [8, 512], F32, kind="ExternalOutput")
        dbg_gsc = nc.dram_tensor("dbg_gsc", [128, 128], F32, kind="ExternalOutput")
        dbg_h2f = nc.dram_tensor("dbg_h2f", [128, 1024], F32, kind="ExternalOutput")

    base = (nc.sbuf_base + 95) // 32 * 32
    MAP_BYTES = F0 + 4096

    def sb(name, shape, dtype, offset):
        return nc.alloc_sbuf_tensor_at(name, shape, dtype,
                                       offset=base + offset)
    x_sb = sb("x_sb", [128, 16384], F32, offset=A_X)
    h_sb = sb("h_sb", [128, 16384], BF16, offset=B_H)
    w1q = [sb("w1qA", [128, 8192], BF16, offset=C0),
           sb("w1qB", [128, 8192], BF16, offset=C0 + 16384)]
    w2q = [sb("w2qA", [128, 8192], BF16, offset=C0 + 32768),
           sb("w2qB", [128, 8192], BF16, offset=C0 + 49152)]
    exp_sb = sb("exp_sb", [128, 4096], BF16, offset=C_EXP)
    ctx_sb = sb("ctx_sb", [128, 8192], BF16, offset=C_CTX)
    q_sb = sb("q_sb", [128, 2048], BF16, offset=C_Q)
    k_sb = sb("k_sb", [128, 2048], BF16, offset=C_K)
    vplus = sb("vplus", [128, 1088], BF16, offset=C_VP)
    wq_sb = sb("wq_sb", [128, 1024], BF16, offset=C_WQ)
    v_f32 = sb("v_f32", [64, 2048], F32, offset=C_VF)
    recip = sb("recip", [1, 1024], F32, offset=C_RECIP)
    bca_sb = sb("bca_sb", [64, 1024], F32, offset=C_BCA)
    hsc = sb("hsc", [128, 8192], BF16, offset=D_HSC)
    woT_sb = sb("woT_sb", [128, 8192], BF16, offset=D_WO)
    h2f = sb("h2f", [128, 1024], F32, offset=D_H2F)
    xsq = sb("xsq", [128, 1024], F32, offset=D_XSQ)
    mrow = sb("mrow", [1, 512], F32, offset=D_ROWS)
    vrow = sb("vrow", [1, 512], F32, offset=D_ROWS + 2048)
    brow = sb("brow", [1, 512], F32, offset=D_ROWS + 4096)
    wbc = sb("wbc", [128, 2048], F32, offset=E_WBC)
    wfm = sb("wfm", [8, 2048], F32, offset=E_WFM)
    b2_sb = sb("b2_sb", [8, 1024], F32, offset=E_B2)
    bc8_sb = sb("bc8_sb", [8, 1024], F32, offset=E_BC8)
    b1_sb = sb("b1_sb", [128, 256], F32, offset=E_B1)

    fofs = [F0]

    def fal(name, shape, dtype):
        h = sb(name, list(shape), dtype, offset=fofs[0])
        per = int(np.prod(shape[1:])) * (2 if dtype == BF16 else 4)
        fofs[0] += (per + 31) // 32 * 32
        return h

    _resv_bytes = base + MAP_BYTES - nc.sbuf_base
    nc.alloc_sbuf_tensor("manual_map_reserved", [128, _resv_bytes],
                         mybir.dt.uint8)

    wg_sb = fal("wg_sb", (128, 64), F32)
    wkT_sb = fal("wkT_sb", (128, 512), BF16)
    wvT_sb = fal("wvT_sb", (128, 512), BF16)
    bq_sb = fal("bq_sb", (128, 8), F32)
    bqd_sb = fal("bqd_sb", (128, 16), F32)
    bo_sb = fal("bo_sb", (128, 8), F32)
    bv_sb = fal("bv_sb", (64, 1), F32)
    onec_sb = fal("onec_sb", (128, 1), F32)
    n20_sb = fal("n20_sb", (128, 1), F32)
    eps_sb = fal("eps_sb", (128, 1), F32)
    ones1_sb = fal("ones1_sb", (1, 128), F32)
    id_sb = fal("id_sb", (128, 128), F32)
    gsc = fal("gsc", (128, 128), F32)
    ones128_r = fal("ones128_r", (128, 128), F32R)
    x_r = sb("x_r", [128, 16384], F32R, offset=A_X)
    xsq_r = sb("xsq_r", [128, 1024], F32R, offset=D_XSQ)
    vrow_r = sb("vrow_r", [1, 512], F32R, offset=D_ROWS + 2048)
    brow_r = sb("brow_r", [1, 512], F32R, offset=D_ROWS + 4096)
    recip_r = sb("recip_r", [1, 1024], F32R, offset=C_RECIP)
    wfm_r = sb("wfm_r", [8, 2048], F32R, offset=E_WFM)
    bc8_r = sb("bc8_r", [8, 1024], F32R, offset=E_BC8)
    b2_r = sb("b2_r", [8, 1024], F32R, offset=E_B2)

    assert base + fofs[0] <= nc.sbuf_top, (
        f"SBUF overflow: {base + fofs[0]} > {nc.sbuf_top}")

    pall = nc.alloc_psum_tensor("pall", [128, 4096], F32)

    P = Plan()
    lds = P.sem("lds")
    ldw = P.sem("ldw")
    mm = P.sem("mm")
    dve = P.sem("dve")
    act = P.sem("act")
    st = P.sem("st")

    def confirm(src_ap, rows=1):
        dump = gsc[0:rows, 30:31]
        P.op("vector", lambda e, a=src_ap, o=dump: e.tensor_scalar_mul(
            o, a, 1.0), inc=(dve, 1))
        return dve.count

    S = dict(wo_done=0, wo_done_dve=0, ctx_free_dve=0, bcast_free_dve=0,
             qbank_free_mm=0, moe_all_done=0, moe_evac_done=0, h_free_mm=0,
             woT_cnt=0, attn_dve_done=0, xsq_free_mm=0)
    if dbg_taps:
        S["tap1"] = None
    x_ready = [0, 0]

    def ld(out_ap, in_ap, sem=lds, eng="sync", waits=()):
        P.op(eng, lambda e, o=out_ap, i=in_ap: e.dma_start(out=o, in_=i),
             waits=waits, inc=(sem, 16))

    ld(bq_sb[:, :], bqt.ap())
    ld(bqd_sb[:, :], bqd.ap())
    ld(bo_sb[:, :], bot.ap())
    ld(bv_sb[:, :], bvv.ap())
    ld(onec_sb[:, :], onecol.ap())
    ld(n20_sb[:, :], neg20.ap())
    ld(eps_sb[:, :], epsc.ap())
    ld(ones1_sb[:, :], ones1.ap())
    ld(id_sb[:, :], id128.ap())
    ld(b1_sb[:, :], b1t.ap())
    ld(b2_sb[:, :], b2.ap())
    ld(bc8_sb[:, :], bc8.ap())
    ld(wg_sb[:, :], dap(wg, 0, [(8, 128), (1024, 8), (1, 8)]))
    consts_done = lds.count
    ld(ones128_r[:, :], ones128.ap(), ldw, "gpsimd")
    ld(wkT_sb[:, :], dap(wkT, 0, [(64, 128), (8192, 8), (1, 64)]), ldw, "gpsimd")
    ld(wvT_sb[:, :], dap(wvT, 0, [(64, 128), (8192, 8), (1, 64)]), ldw, "gpsimd")
    kv_w_cnt = ldw.count

    def x_load(b, waits):
        in_ap = dap(xT, b * D * T, [(2048, 128), (262144, 8), (1, 2048)])
        ld(x_sb[:, :], in_ap, lds, "sync", waits)
        return lds.count

    def wq_load(h, waits):
        in_ap = dap(wqT, h * 64, [(1024, 128), (131072, 8), (1, 64)])
        ld(wq_sb[:, (h % 2) * 512:(h % 2) * 512 + 512], in_ap, ldw, "gpsimd",
           waits)
        return ldw.count

    def emit_ln(b, gate):
        for tau in range(NTAU):
            tbase = tau * 512
            if gate:
                xw = [(dve, S["wo_done_dve"]), (mm, S["wo_done"])]
            else:
                xw = [(lds, x_ready[b]), (mm, S["moe_all_done"]),
                      (dve, S["moe_evac_done"])]
            sq_cnt = {}
            s2m_cnt = {}
            s1_done = 0
            for c in range(NCH):
                xs = x_sb[:, c * 2048 + tbase: c * 2048 + tbase + 512]
                w = list(xw)
                w.append((mm, s2m_cnt[c - 2] if c >= 2
                          else S["xsq_free_mm"]))
                dst = xsq[:, (c % 2) * 512:(c % 2) * 512 + 512]
                P.op("scalar", lambda e, o=dst, i=xs: e.activation(
                    o, i, AF.Square), waits=w, inc=(act, 1))
                sq_cnt[c] = act.count
                xs_r = x_r[:, c * 2048 + tbase: c * 2048 + tbase + 512]
                dst_r = xsq_r[:, (c % 2) * 512:(c % 2) * 512 + 512]
                P.op("tensor", lambda e, r=xs_r, fst=(c == 0), lst=(c == 7):
                     e.matmul(pall[0:128, 0:512], ones128_r[:, :], r,
                              start=fst, stop=lst),
                     waits=(list(xw) + [(lds, consts_done),
                                        (ldw, kv_w_cnt)]) if c == 0 else [],
                     inc=(mm, 1) if c == 7 else None)
                if c == 7:
                    s1_done = mm.count
                P.op("tensor", lambda e, r=dst_r, fst=(c == 0), lst=(c == 7):
                     e.matmul(pall[0:128, 512:1024], ones128_r[:, :], r,
                              start=fst, stop=lst),
                     waits=[(act, sq_cnt[c])], inc=(mm, 1))
                s2m_cnt[c] = mm.count
            s2_done = mm.count
            S["xsq_free_mm"] = s2_done
            P.op("vector", lambda e: e.tensor_scalar_mul(
                mrow[0:1, :], pall[0:1, 0:512], 1.0 / 1024.0),
                waits=[(mm, s1_done)])
            P.op("vector", lambda e: e.tensor_mul(
                brow[0:1, :], mrow[0:1, :], mrow[0:1, :]))
            P.op("vector", lambda e: e.scalar_tensor_tensor(
                vrow[0:1, :], pall[0:1, 512:1024], 1.0 / 1024.0, brow[0:1, :],
                ALU.mult, ALU.subtract), waits=[(mm, s2_done)], inc=(dve, 1))
            var_done = dve.count
            P.op("scalar", lambda e: e.activation(
                brow[0:1, :], vrow[0:1, :], AF.Ln, bias=eps_sb[0:1, 0:1]),
                waits=[(dve, var_done)], inc=(act, 1))
            ln_cnt = act.count
            P.op("vector", lambda e: e.tensor_scalar_mul(
                brow[0:1, :], brow[0:1, :], -0.5),
                waits=[(act, ln_cnt)], inc=(dve, 1))
            hl_cnt = dve.count
            P.op("scalar", lambda e: e.activation(
                vrow[0:1, :], brow[0:1, :], AF.Exp),
                waits=[(dve, hl_cnt)], inc=(act, 1))
            a_cnt = act.count
            P.op("vector", lambda e: e.scalar_tensor_tensor(
                brow[0:1, :], mrow[0:1, :], -1.0, vrow[0:1, :],
                ALU.mult, ALU.mult), waits=[(act, a_cnt)], inc=(dve, 1))
            P.op("vector", lambda e: e.scalar_tensor_tensor(
                gsc[0:1, 30:31], vrow[0:1, 511:512], 1.0,
                brow[0:1, 511:512], ALU.mult, ALU.add), inc=(dve, 1))
            b_cnt = dve.count
            P.op("tensor", lambda e: e.matmul(
                pall[0:128, 1024:1536], ones128_r[0:1, :], vrow_r[0:1, :],
                start=True, stop=True),
                waits=[(act, a_cnt), (dve, b_cnt)], inc=(mm, 1))
            P.op("tensor", lambda e: e.matmul(
                pall[0:128, 1536:2048], ones128_r[0:1, :], brow_r[0:1, :],
                start=True, stop=True), inc=(mm, 1))
            ab_done = mm.count
            h2f_free = {0: 0, 1: 0}
            gate_grp = 0
            for c in range(NCH):
                xs = x_sb[:, c * 2048 + tbase: c * 2048 + tbase + 512]
                hsl = h_sb[:, c * 2048 + tbase: c * 2048 + tbase + 512]
                slot = h2f[:, (c % 2) * 512:(c % 2) * 512 + 512]
                w = [(mm, ab_done), (mm, h2f_free[c % 2])]
                if not gate:
                    w.append((mm, S["h_free_mm"]))
                P.op("vector", lambda e, o=slot, i=xs: e.tensor_tensor(
                    o, i, pall[:, 1024:1536], ALU.mult), waits=w)
                if gate:
                    P.op("vector", lambda e, o=slot: e.tensor_tensor(
                        o, o, pall[:, 1536:2048], ALU.add))
                    P.op("vector", lambda e, o=hsl, i=slot: e.tensor_copy(
                        o, i), inc=(dve, 1))
                    hdone = dve.count
                    P.op("tensor", lambda e, r=slot,
                         lw=wg_sb[:, c * 8:(c + 1) * 8],
                         fst=(c == 0), lst=(c == 7): e.matmul(
                             pall[0:8, 2048:2560], lw, r, start=fst, stop=lst),
                         waits=[(dve, hdone)], inc=(mm, 1))
                    h2f_free[c % 2] = mm.count
                    gate_grp = mm.count
                else:
                    P.op("vector", lambda e, o=hsl, i=slot: e.tensor_tensor(
                        o, i, pall[:, 1536:2048], ALU.add), inc=(dve, 1))
            if gate:
                emit_top2(tau, gate_grp)
        return dve.count

    def emit_top2(tau, gate_grp):
        P.op("vector", lambda e: e.tensor_copy(
            xsq[0:8, 0:512], pall[0:8, 2048:2560]),
            waits=[(mm, gate_grp)], inc=(dve, 1))
        gevac = confirm(xsq[0:8, 511:512], rows=8)
        if dbg_taps and S.get("tap1", 1) is None:
            ld(dbg_gl.ap(), xsq[0:8, 0:512], st, "sync", [(dve, gevac)])
            ld(dbg_h2f.ap(), h2f[:, :], st, "sync", [])
        for j in range(4):
            P.op("tensor", lambda e, j=j: e.transpose(
                pall[0:128, 2560 + j * 8: 2568 + j * 8],
                xsq[0:8, j * 128:(j + 1) * 128], id_sb[0:8, 0:8]),
                waits=[(dve, gevac)] if j == 0 else [],
                inc=(mm, 1) if j == 3 else None)
        tp_done = mm.count
        S["xsq_free_mm"] = tp_done

        def g(j):
            return pall[0:128, 2560 + j * 8: 2568 + j * 8]

        def vop(fn, waits=()):
            P.op("vector", fn, waits=list(waits) + [(dve, dve.count)],
                 inc=(dve, 1))
            return dve.count

        for j in range(4):
            vop(lambda e, j=j: e.reduce_max(gsc[:, j:j + 1], g(j), axis=AX.X),
                [(mm, tp_done)] if j == 0 else [])
        for j in range(4):
            vop(lambda e, j=j: e.tensor_scalar(
                gsc[:, 32 + j * 8:40 + j * 8], g(j), gsc[:, j:j + 1],
                None, ALU.is_equal))
        for j in range(4):
            vop(lambda e, j=j: e.scalar_tensor_tensor(
                gsc[:, 64 + j * 8:72 + j * 8], gsc[:, 32 + j * 8:40 + j * 8],
                -1e30, g(j), ALU.mult, ALU.add))
        for j in range(4):
            vop(lambda e, j=j: e.reduce_max(
                gsc[:, 4 + j:5 + j], gsc[:, 64 + j * 8:72 + j * 8],
                axis=AX.X))
        d_done = vop(lambda e: e.tensor_sub(
            gsc[:, 8:12], gsc[:, 4:8], gsc[:, 0:4]))
        P.op("scalar", lambda e: e.activation(
            gsc[:, 12:16], gsc[:, 8:12], AF.Exp),
            waits=[(dve, d_done)], inc=(act, 1))
        e_done = act.count
        vop(lambda e: e.tensor_scalar_add(
            gsc[:, 16:20], gsc[:, 12:16], 1.0), [(act, e_done)])
        vop(lambda e: e.reciprocal(gsc[:, 20:24], gsc[:, 16:20]))
        vop(lambda e: e.tensor_mul(
            gsc[:, 24:28], gsc[:, 12:16], gsc[:, 20:24]))
        for j in range(4):
            vop(lambda e, j=j: e.tensor_scalar(
                gsc[:, 64 + j * 8:72 + j * 8], g(j), gsc[:, 4 + j:5 + j],
                None, ALU.is_equal))
        for j in range(4):
            vop(lambda e, j=j: e.tensor_scalar(
                gsc[:, 32 + j * 8:40 + j * 8], gsc[:, 32 + j * 8:40 + j * 8],
                gsc[:, 20 + j:21 + j], None, ALU.mult))
        for j in range(4):
            vop(lambda e, j=j: e.scalar_tensor_tensor(
                gsc[:, 96 + j * 8:104 + j * 8], gsc[:, 64 + j * 8:72 + j * 8],
                gsc[:, 24 + j:25 + j], gsc[:, 32 + j * 8:40 + j * 8],
                ALU.mult, ALU.add))
        w8_done = confirm(gsc[:, 127:128], rows=128)
        if dbg_taps and S.get("tap1", 1) is None:
            ld(dbg_gsc.ap(), gsc[:, :], st, "sync", [(dve, w8_done)])
            S["tap1"] = True
        for j in range(4):
            P.op("tensor", lambda e, j=j: e.transpose(
                pall[0:8, 2048 + j * 128: 2176 + j * 128],
                gsc[:, 96 + j * 8:104 + j * 8], id_sb[:, :]),
                waits=[(dve, w8_done)] if j == 0 else [],
                inc=(mm, 1) if j == 3 else None)
        w8t = mm.count
        for j in range(4):
            P.op("vector", lambda e, j=j, t=tau: e.tensor_copy(
                wfm[0:8, t * 512 + j * 128: t * 512 + (j + 1) * 128],
                pall[0:8, 2048 + j * 128: 2176 + j * 128]),
                waits=[(mm, w8t)] if j == 0 else [],
                inc=(dve, 1) if j == 3 else None)
        confirm(wfm[0:8, tau * 512 + 511: tau * 512 + 512], rows=8)

    def emit_kv(b, ln_done):
        kevac = 0
        for tau in range(NTAU):
            tbase = tau * 512
            for half in range(2):
                for c in range(NCH):
                    w = []
                    if half == 0 and c == 0:
                        w = [(dve, ln_done), (ldw, kv_w_cnt), (dve, kevac)]
                    P.op("tensor", lambda e, c=c, t=tbase, hf=half,
                         fst=(c == 0), lst=(c == 7): e.matmul(
                             pall[hf * 64:hf * 64 + 64, 0:512],
                             wkT_sb[:, c * 64:(c + 1) * 64],
                             h_sb[:, c * 2048 + t:c * 2048 + t + 512],
                             start=fst, stop=lst,
                             tile_position=(0, hf * 64)),
                         waits=w,
                         inc=(mm, 1) if (half == 1 and c == 7) else None)
            kg = mm.count
            for c in range(NCH):
                P.op("tensor", lambda e, c=c, t=tbase, fst=(c == 0),
                     lst=(c == 7): e.matmul(
                         pall[0:64, 512:1024], wvT_sb[:, c * 64:(c + 1) * 64],
                         h_sb[:, c * 2048 + t:c * 2048 + t + 512],
                         start=fst, stop=lst),
                     inc=(mm, 1) if c == 7 else None)
            vg = mm.count
            P.op("vector", lambda e, t=tbase: e.tensor_copy(
                k_sb[0:128, t:t + 512], pall[0:128, 0:512]),
                waits=[(mm, kg)], inc=(dve, 1))
            P.op("vector", lambda e, t=tbase: e.tensor_scalar_add(
                v_f32[0:64, t:t + 512], pall[0:64, 512:1024],
                bv_sb[0:64, 0:1]), waits=[(mm, vg)], inc=(dve, 1))
            kevac = dve.count
        v_done = dve.count
        for s in range(NSIG):
            P.op("tensor", lambda e, s=s: e.transpose(
                pall[0:128, 1024 + s * 64:1088 + s * 64],
                v_f32[0:64, s * 128:(s + 1) * 128], id_sb[0:64, 0:64]),
                waits=[(dve, v_done)] if s == 0 else [],
                inc=(mm, 1) if s == 15 else None)
        vt_done = mm.count
        for s in range(NSIG):
            P.op("vector", lambda e, s=s: e.tensor_copy(
                vplus[:, s * 68:s * 68 + 64],
                pall[0:128, 1024 + s * 64:1088 + s * 64]),
                waits=[(mm, vt_done)] if s == 0 else [])
        P.op("vector", lambda e: e.memset(
            bass.AP(vplus, 64, [[1088, 128], [68, 16], [1, 1]]), 1.0),
            inc=(dve, 1))
        return confirm(vplus[:, 1084:1085], rows=128)

    def emit_qproj(b, tp, h, wq_cnt, extra):
        for t2 in range(2):
            for half in range(2):
                for kc in range(NCH):
                    w = []
                    if t2 == 0 and half == 0 and kc == 0:
                        w = [(ldw, wq_cnt)] + list(extra)
                    P.op("tensor", lambda e, t2=t2, kc=kc, h=h, tp=tp,
                         hf=half, fst=(kc == 0), lst=(kc == 7): e.matmul(
                             pall[hf * 64:hf * 64 + 64,
                                  3072 + t2 * 512:3584 + t2 * 512],
                             wq_sb[:, (h % 2) * 512 + kc * 64:
                                   (h % 2) * 512 + (kc + 1) * 64],
                             h_sb[:, kc * 2048 + tp * 1024 + t2 * 512:
                                  kc * 2048 + tp * 1024 + (t2 + 1) * 512],
                             start=fst, stop=lst,
                             tile_position=(0, hf * 64)),
                         waits=w,
                         inc=(mm, 1) if (t2 == 1 and half == 1 and kc == 7)
                         else None)
        qp = mm.count
        bq_ap = bqd_sb[:, h:h + 1]
        P.op("vector", lambda e, h=h, bq=bq_ap: e.tensor_scalar(
            q_sb[0:128, (h % 2) * 1024:(h % 2 + 1) * 1024],
            pall[0:128, 3072:4096], bq, 0.125, ALU.add, ALU.mult),
            waits=[(mm, qp)], inc=(dve, 1))
        qed = confirm(q_sb[0:128, (h % 2) * 1024 + 1023:
                           (h % 2) * 1024 + 1024], rows=128)
        return qp, qed

    def emit_head_tail(tp, h, recip_dve, qe_next):
        for t2 in range(2):
            w = [(dve, recip_dve), (dve, qe_next)] if t2 == 0 else []
            P.op("tensor", lambda e, t2=t2: e.matmul(
                pall[0:128, 3072 + t2 * 512:3584 + t2 * 512],
                ones128_r[0:1, :], recip_r[0:1, t2 * 512:(t2 + 1) * 512],
                start=True, stop=True),
                waits=w, inc=(mm, 1) if t2 == 1 else None)
        bc = mm.count
        P.op("vector", lambda e: e.tensor_copy(
            bca_sb[0:64, :], pall[0:64, 3072:4096]),
            waits=[(mm, bc)], inc=(dve, 1))
        S["bcast_free_dve"] = dve.count
        P.op("vector", lambda e, h=h: e.tensor_tensor(
            ctx_sb[(h % 2) * 64:(h % 2) * 64 + 64,
                   (h // 2) * 1024:(h // 2 + 1) * 1024],
            pall[0:64, 2048:3072], bca_sb[0:64, :], ALU.mult),
            inc=(dve, 1))
        S["ctx_free_dve"] = dve.count

    def emit_wo(b, tp):
        gi = 0
        evac = {}
        for t2 in range(2):
            for c in range(NCH):
                bank = 2048 + (gi % 2) * 512
                for kc in range(NCH):
                    w = []
                    if kc == 0:
                        w = [(dve, S["ctx_free_dve"]), (ldw, S["woT_cnt"])]
                        if gi >= 2:
                            w.append((dve, evac[gi - 2]))
                    P.op("tensor", lambda e, kc=kc, c=c, t2=t2, bank=bank,
                         fst=(kc == 0), lst=(kc == 7): e.matmul(
                             pall[0:128, bank:bank + 512],
                             woT_sb[:, kc * 1024 + c * 128:
                                    kc * 1024 + (c + 1) * 128],
                             ctx_sb[:, kc * 1024 + t2 * 512:
                                    kc * 1024 + (t2 + 1) * 512],
                             start=fst, stop=lst),
                         waits=w, inc=(mm, 1) if kc == 7 else None)
                grp = mm.count
                xsl = x_sb[:, c * 2048 + tp * 1024 + t2 * 512:
                           c * 2048 + tp * 1024 + (t2 + 1) * 512]
                P.op("vector", lambda e, bank=bank, xs=xsl, c=c:
                     e.scalar_tensor_tensor(
                         xs, pall[0:128, bank:bank + 512],
                         bo_sb[:, c:c + 1], xs, ALU.add, ALU.add),
                     waits=[(mm, grp)], inc=(dve, 1))
                evac[gi] = dve.count
                gi += 1
        S["wo_done"] = mm.count
        S["wo_done_dve"] = dve.count
        S["qbank_free_mm"] = mm.count

    def emit_attention(b, kv_done, ln_done):
        lastexp = [0, 0]
        qe = {}
        qp = {}
        wq_cnt = {0: wq_cnt_head0[b], 1: wq_cnt_head1[b]}
        pend_tail = None
        qp0, qe0 = emit_qproj(b, 0, 0, wq_cnt[0],
                              [(dve, ln_done), (dve, S["bcast_free_dve"]),
                               (mm, S["qbank_free_mm"])])
        qp[0], qe[0] = qp0, qe0
        g_idx = 0
        for tp in range(2):
            for h in range(H):
                seqi = tp * H + h
                qe_h = qe[seqi]
                if seqi + 2 < 2 * H:
                    wq_cnt[seqi + 2] = wq_load((seqi + 2) % H,
                                               [(mm, qp[seqi])])
                exp_done = {}

                def emit_qk(s):
                    slot = s % 2
                    base = slot * 1024
                    rg = (s % 2) * 64
                    for t2 in range(2):
                        w = []
                        if t2 == 0:
                            w = [(act, lastexp[slot]), (dve, qe_h),
                                 (dve, kv_done)]
                        P.op("tensor", lambda e, s=s, t2=t2, h=h,
                             base=base, rg=rg: e.matmul(
                                 pall[0:128, base + t2 * 512:
                                      base + (t2 + 1) * 512],
                                 k_sb[rg:rg + 64, s * 128:(s + 1) * 128],
                                 q_sb[rg:rg + 64,
                                      (h % 2) * 1024 + t2 * 512:
                                      (h % 2) * 1024 + (t2 + 1) * 512],
                                 start=True, stop=True,
                                 tile_position=(rg, 0)),
                             waits=w, inc=(mm, 1) if t2 == 1 else None)
                    qk = mm.count
                    P.op("scalar", lambda e, base=base, s=s: e.activation(
                        exp_sb[:, (s % 4) * 1024:(s % 4 + 1) * 1024],
                        pall[:, base:base + 1024], AF.Exp,
                        bias=n20_sb[:, 0:1]),
                        waits=[(mm, qk)], inc=(act, 1))
                    lastexp[slot] = act.count
                    exp_done[s] = act.count

                def emit_pv(sp):
                    for t2 in range(2):
                        w = []
                        if t2 == 0:
                            w = [(act, exp_done[sp])]
                            if sp == 0:
                                w += [(dve, S["ctx_free_dve"]),
                                      (mm, S["wo_done"]),
                                      (dve, kv_done)]
                        P.op("tensor", lambda e, sp=sp, t2=t2: e.matmul(
                            pall[0:65, 2048 + t2 * 512:
                                 2048 + (t2 + 1) * 512],
                            vplus[:, sp * 68:sp * 68 + 65],
                            exp_sb[:, (sp % 4) * 1024 + t2 * 512:
                                   (sp % 4) * 1024 + (t2 + 1) * 512],
                            start=(sp == 0), stop=(sp == NSIG - 1)),
                            waits=w,
                            inc=(mm, 1) if (sp == NSIG - 1 and t2 == 1)
                            else None)

                for i in range(9):
                    if i < 8:
                        emit_qk(2 * i)
                        emit_qk(2 * i + 1)
                    if i == 1 and pend_tail is not None:
                        ptp, ph, prd, pseq = pend_tail
                        emit_head_tail(ptp, ph, prd, qe[pseq + 1])
                        pend_tail = None
                    if i >= 1:
                        emit_pv(2 * (i - 1))
                        emit_pv(2 * (i - 1) + 1)
                    if i == 7 and seqi + 1 < 2 * H:
                        tpn, hn = divmod(seqi + 1, H)
                        qpn, qen = emit_qproj(b, tpn, hn, wq_cnt[seqi + 1],
                                              [(dve, S["bcast_free_dve"])])
                        qp[seqi + 1], qe[seqi + 1] = qpn, qen
                g_idx += 16
                pv_done = mm.count
                P.op("vector", lambda e: e.reciprocal(
                    recip[0:1, :], pall[64:65, 2048:3072]),
                    waits=[(mm, pv_done)], inc=(dve, 1))
                pend_tail = (tp, h, dve.count, seqi)
            ptp, ph, prd, pseq = pend_tail
            emit_head_tail(ptp, ph, prd, qe.get(pseq + 1, qe[pseq]))
            pend_tail = None
            emit_wo(b, tp)
        S["attn_dve_done"] = dve.count

    def emit_moe(b, gate_done):
        pair_loaded = {}

        def prefetch(p, waits):
            e_, qr = divmod(p, NQR)
            slot = p % 2
            ap1 = dap(w1, e_ * D * DFF + qr * 1024,
                      [(4096, 128), (524288, 8), (1, 1024)])
            P.op("gpsimd", lambda e, o=w1q[slot][:, :], i=ap1:
                 e.dma_start(out=o, in_=i), waits=waits, inc=(ldw, 16))
            ap2 = dap(w2, e_ * DFF * D + qr * 1024 * 1024,
                      [(1024, 128), (131072, 8), (1, 1024)])
            P.op("gpsimd", lambda e, o=w2q[slot][:, :], i=ap2:
                 e.dma_start(out=o, in_=i), inc=(ldw, 16))
            pair_loaded[p] = ldw.count

        prefetch(0, [(mm, S["wo_done"]), (dve, S["attn_dve_done"])])
        prefetch(1, [])

        def bcast_mm(e_, waits):
            for tau in range(NTAU):
                P.op("tensor", lambda e, e_=e_, tau=tau: e.matmul(
                    pall[0:128, 2048 + tau * 512:2048 + (tau + 1) * 512],
                    bc8_r[0:8, e_ * 128:(e_ + 1) * 128],
                    wfm_r[0:8, tau * 512:(tau + 1) * 512],
                    start=True, stop=True),
                    waits=waits if tau == 0 else [],
                    inc=(mm, 1) if tau == 3 else None)
            return mm.count

        def bcast_copy(cnt):
            for tau in range(NTAU):
                P.op("vector", lambda e, tau=tau: e.tensor_copy(
                    wbc[:, tau * 512:(tau + 1) * 512],
                    pall[0:128, 2048 + tau * 512:2048 + (tau + 1) * 512]),
                    waits=[(mm, cnt)] if tau == 0 else [],
                    inc=(dve, 1) if tau == 3 else None)

        bcast_copy(bcast_mm(0, [(dve, gate_done)]))
        pending_bcast = None
        pend_w2 = None
        h_free = {0: 0, 1: 0}
        hbank_free = {0: 0, 1: 0}
        obank_free = {0: 0, 1: 0}
        last_w1 = 0
        for u in range(E * NQR * NTAU):
            e_ = u // (NQR * NTAU)
            qr = (u // NTAU) % NQR
            tau = u % NTAU
            p = e_ * NQR + qr
            slot = p % 2
            hslot = u % 2
            stt7 = 0
            for fi in range(8):
                bank = (fi % 2) * 512
                for kc in range(NCH):
                    w = []
                    if kc == 0:
                        w = [(ldw, pair_loaded[p]),
                             (dve, hbank_free[fi % 2])]
                        if fi == 0:
                            w.append((mm, h_free[hslot]))
                            if u == 0:
                                w.append((dve, gate_done))
                    P.op("tensor", lambda e, kc=kc, fi=fi, sl=slot, tau=tau,
                         bank=bank, fst=(kc == 0), lst=(kc == 7): e.matmul(
                             pall[0:128, bank:bank + 512],
                             w1q[sl][:, kc * 1024 + fi * 128:
                                     kc * 1024 + (fi + 1) * 128],
                             h_sb[:, kc * 2048 + tau * 512:
                                  kc * 2048 + (tau + 1) * 512],
                             start=fst, stop=lst),
                         waits=w, inc=(mm, 1) if kc == 7 else None)
                grp = mm.count
                last_w1 = grp
                hso = hslot * 4096 + fi * 512
                b1ap = b1_sb[:, e_ * 32 + qr * 8 + fi:
                             e_ * 32 + qr * 8 + fi + 1]
                P.op("vector", lambda e, bank=bank, hso=hso, b1ap=b1ap,
                     tau=tau: e.scalar_tensor_tensor(
                         hsc[:, hso:hso + 512], pall[0:128, bank:bank + 512],
                         b1ap, wbc[:, tau * 512:(tau + 1) * 512],
                         ALU.add, ALU.mult), waits=[(mm, grp)])
                P.op("vector", lambda e, hso=hso: e.tensor_scalar_max(
                    hsc[:, hso:hso + 512], hsc[:, hso:hso + 512], 0.0),
                    inc=(dve, 1))
                hbank_free[fi % 2] = dve.count
                if fi == 7:
                    stt7 = confirm(hsc[:, hso + 511:hso + 512], rows=128)
            if qr == NQR - 1 and tau == NTAU - 1 and pending_bcast is not None:
                bcast_copy(pending_bcast)
                pending_bcast = None

            def emit_w2(u2, stt7_2, slot2, hslot2):
                e2 = u2 // (NQR * NTAU)
                qr2 = (u2 // NTAU) % NQR
                tau2 = u2 % NTAU
                for c in range(NCH):
                    ob = (tau2 * 8 + c) % 2
                    gbank = 1024 + ob * 512
                    seeded = (e2 == 0 and qr2 == 0)
                    if seeded:
                        P.op("tensor", lambda e, c=c, tau=tau2, gbank=gbank:
                             e.matmul(
                                 pall[0:128, gbank:gbank + 512],
                                 b2_r[0:8, c * 128:(c + 1) * 128],
                                 wfm_r[0:8, tau * 512:(tau + 1) * 512],
                                 start=True, stop=False),
                             waits=[(dve, stt7_2), (dve, obank_free[ob])])
                    for fc in range(8):
                        w = []
                        if fc == 0 and not seeded:
                            w = [(dve, stt7_2), (dve, obank_free[ob])]
                        P.op("tensor", lambda e, fc=fc, c=c, sl=slot2,
                             hs0=hslot2 * 4096, gbank=gbank,
                             fst=(fc == 0 and not seeded), lst=(fc == 7):
                             e.matmul(
                                 pall[0:128, gbank:gbank + 512],
                                 w2q[sl][:, fc * 1024 + c * 128:
                                         fc * 1024 + (c + 1) * 128],
                                 hsc[:, hs0 + fc * 512:hs0 + (fc + 1) * 512],
                                 start=fst, stop=lst),
                             waits=w, inc=(mm, 1) if fc == 7 else None)
                    grp = mm.count
                    xsl = x_sb[:, c * 2048 + tau2 * 512:
                               c * 2048 + (tau2 + 1) * 512]
                    P.op("vector", lambda e, gbank=gbank, xs=xsl:
                         e.tensor_add(
                             xs, xs, pall[0:128, gbank:gbank + 512]),
                         waits=[(mm, grp)], inc=(dve, 1))
                    obank_free[ob] = dve.count
                h_free[hslot2] = mm.count
                p2 = e2 * NQR + qr2
                if tau2 == NTAU - 1 and p2 + 2 < E * NQR:
                    prefetch(p2 + 2, [(mm, mm.count)])

            if pend_w2 is not None:
                emit_w2(*pend_w2)
            pend_w2 = (u, stt7, slot, hslot)
            if qr == NQR - 1 and tau == NTAU - 2 and e_ < E - 1:
                pending_bcast = bcast_mm(e_ + 1, [(dve, dve.count)])
        emit_w2(*pend_w2)
        S["moe_all_done"] = mm.count
        S["moe_evac_done"] = dve.count
        S["h_free_mm"] = last_w1

    wq_cnt_head0 = {}
    wq_cnt_head1 = {}
    for b in range(NB):
        if b == 0:
            x_ready[0] = x_load(0, [])
        else:
            x_ready[b] = x_load(b, [(st, st.count)])
        wq_cnt_head0[b] = wq_load(0, [(mm, S["qbank_free_mm"]),
                                      (mm, S["moe_all_done"])])
        wq_cnt_head1[b] = wq_load(1, [])
        ln1 = emit_ln(b, gate=False)
        P.op("gpsimd", lambda e: e.dma_start(
            out=woT_sb[:, :],
            in_=dap(woT, 0, [(1024, 128), (131072, 8), (1, 1024)])),
            waits=[(dve, ln1), (mm, S["moe_all_done"])], inc=(ldw, 16))
        S["woT_cnt"] = ldw.count
        kv_done = emit_kv(b, ln1)
        emit_attention(b, kv_done, ln1)
        if dbg_taps and b == 0:
            for c in range(NCH):
                ld(dap(dbg_xa, c * 128 * T, [(2048, 128), (1, 2048)]),
                   x_sb[:, c * 2048:(c + 1) * 2048], st, "sync",
                   [(dve, S["wo_done_dve"])] if c == 0 else [])
        gate_done = emit_ln(b, gate=True)
        if dbg_taps and b == 0:
            for c in range(NCH):
                ld(dap(dbg_h2, c * 128 * T, [(2048, 128), (1, 2048)]),
                   h_sb[:, c * 2048:(c + 1) * 2048], st, "gpsimd",
                   [(dve, gate_done)] if c == 0 else [])
            ld(dbg_wfm.ap(), wfm[:, :], st, "sync", [(dve, gate_done)])
        emit_moe(b, gate_done)
        for c in range(NCH):
            out_ap = dap(outT, b * D * T + c * 128 * T, [(2048, 128), (1, 2048)])
            P.op("sync", lambda e, o=out_ap, c=c: e.dma_start(
                out=o, in_=x_sb[:, c * 2048:(c + 1) * 2048]),
                waits=[(dve, S["moe_evac_done"])] if c == 0 else [],
                inc=(st, 16))
    P.op("sync", lambda e: None, waits=[(st, st.count)])

    from contextlib import ExitStack
    with ExitStack() as ctx:
        for s in P.sems:
            s.h = ctx.enter_context(nc.semaphore(s.name))
        block = ctx.enter_context(nc.Block())

        @block.tensor
        def _(t):
            P.emit_engine("tensor", t)

        @block.vector
        def _(v):
            P.emit_engine("vector", v)

        @block.scalar
        def _(sc):
            P.emit_engine("scalar", sc)

        @block.sync
        def _(s_):
            P.emit_engine("sync", s_)

        @block.gpsimd
        def _(g):
            P.emit_engine("gpsimd", g)

    return nc


_cache = {}


def kernel(L0_attn_hidden_states, L0_attn_Wq, L0_attn_bq, L0_attn_Wk,
           L0_attn_Wv, L0_attn_bv, L0_attn_Wo, L0_attn_bo, L0_ffn_Wg,
           L0_ffn_W1, L0_ffn_b1, L0_ffn_W2, L0_ffn_b2):
    x = np.asarray(L0_attn_hidden_states, dtype=np.float32)
    Wq = np.asarray(L0_attn_Wq, np.float32)
    bq = np.asarray(L0_attn_bq, np.float32)
    Wk = np.asarray(L0_attn_Wk, np.float32)
    Wv = np.asarray(L0_attn_Wv, np.float32)
    bv = np.asarray(L0_attn_bv, np.float32)
    Wo = np.asarray(L0_attn_Wo, np.float32)
    bo = np.asarray(L0_attn_bo, np.float32)
    Wg = np.asarray(L0_ffn_Wg, np.float32)
    W1 = np.asarray(L0_ffn_W1, np.float32)
    b1 = np.asarray(L0_ffn_b1, np.float32)
    W2 = np.asarray(L0_ffn_W2, np.float32)
    b2 = np.asarray(L0_ffn_b2, np.float32)

    if "nc" not in _cache:
        _cache["nc"] = build_program()
    nc = _cache["nc"]

    shared = {
        "wqT": np.ascontiguousarray(Wq.T),
        "wkT": np.ascontiguousarray(Wk.T),
        "wvT": np.ascontiguousarray(Wv.T),
        "woT": np.ascontiguousarray(Wo.T),
        "bqt": np.ascontiguousarray(bq.reshape(8, 128).T),
        "bqd": np.ascontiguousarray(
            np.tile(bq.reshape(16, 64).T, (2, 1))),
        "bot": np.ascontiguousarray(bo.reshape(8, 128).T),
        "bvv": np.ascontiguousarray(bv.reshape(64, 1)),
        "wg": np.ascontiguousarray(Wg),
        "w1": np.ascontiguousarray(W1),
        "b1t": np.ascontiguousarray(
            b1.reshape(8, 32, 128).transpose(2, 0, 1).reshape(128, 256)),
        "w2": np.ascontiguousarray(W2),
        "b2": np.ascontiguousarray(b2),
        "ones1": np.ones((1, 128), np.float32),
        "ones128": np.ones((128, 128), np.float32),
        "onecol": np.ones((128, 1), np.float32),
        "neg20": np.full((128, 1), -20.0, np.float32),
        "epsc": np.full((128, 1), 1e-5, np.float32),
        "id128": np.eye(128, dtype=np.float32),
        "bc8": np.repeat(np.eye(8, dtype=np.float32), 128, axis=1),
    }
    in_maps = []
    for c in range(NCORES):
        m = dict(shared)
        m["xT"] = np.ascontiguousarray(x[2 * c:2 * c + 2].transpose(0, 2, 1))
        in_maps.append(m)

    res = run_bass_kernel_spmd(nc, in_maps, core_ids=list(range(NCORES)))
    out = np.concatenate(
        [r["outT"].transpose(0, 2, 1) for r in res.results], axis=0)
    return np.ascontiguousarray(out, dtype=np.float32)
